# version 12
# baseline (speedup 1.0000x reference)
import contextlib
import numpy as np
import ml_dtypes
import concourse.bass as bass
import concourse.mybir as mybir
from concourse.ap import AP as APc
from concourse.bass_core_rust import IndirectOffsetOnAxis
from concourse.bass_utils import run_bass_kernel_spmd

F32 = mybir.dt.float32
BF16 = mybir.dt.bfloat16
I32 = mybir.dt.int32
AF = mybir.ActivationFunctionType
ALU = mybir.AluOpType
AX = mybir.AxisListType
ET = mybir.EngineType

B, L, D, S, E = 4, 4096, 1024, 128, 2048
HALF, NTC, QB = 2048, 16, 1024
EPS1 = 1e-5
INV_SQRT_S = float(1.0 / np.sqrt(128.0))
OOBF = float(1 << 28)
PAD_BF = float(np.array([0x7F7F], np.uint16).view(ml_dtypes.bfloat16)[0])
ENGS = ("sync", "gpsimd", "vector", "tensor", "scalar")
_UPTO = None
_DUMPT = None
BAND_KC = {0: list(range(8)), 1: list(range(16))}


class Pxy:
    def __init__(self, cnt, sems, handle, chain_sem=None):
        self.c = cnt
        self.sems = sems
        self.h = handle
        self.chain_sem = chain_sem

    def dma(self, sem, out, in_, **kw):
        self.c[sem] = self.c.get(sem, 0) + 16
        if self.h is not None:
            self.h.dma_start(out=out, in_=in_, **kw).then_inc(self.sems[sem], 16)

    def ind(self, sem, out, out_off, in_, in_off, bc):
        self.c[sem] = self.c.get(sem, 0) + 16
        if self.h is not None:
            self.h.indirect_dma_start(
                out=out, out_offset=out_off, in_=in_, in_offset=in_off,
                bounds_check=bc, oob_is_err=False,
            ).then_inc(self.sems[sem], 16)

    def op(self, sem, meth, *a, **kw):
        chain = self.chain_sem
        if chain is not None:
            sem = chain
            prev = self.c.get(sem, 0)
        if sem is not None:
            self.c[sem] = self.c.get(sem, 0) + 1
        if self.h is not None:
            if chain is not None and prev > 0:
                self.h.wait_ge(self.sems[sem], prev)
            ins = getattr(self.h, meth)(*a, **kw)
            if sem is not None:
                ins.then_inc(self.sems[sem], 1)

    def wait(self, sem, val=None):
        if self.h is not None:
            v = self.c.get(sem, 0) if val is None else val
            if v > 0:
                self.h.wait_ge(self.sems[sem], v)

    def fence(self, sem):
        if self.chain_sem is not None:
            sem = self.chain_sem
        self.c[sem] = self.c.get(sem, 0) + 1
        if self.h is not None:
            self.h.drain().then_inc(self.sems[sem], 1)


def rr(ap, spec, **kw):
    return ap.rearrange(spec, **kw)


def build():
    nc = bass.Bass("TRN2", target_bir_lowering=False,
                   detect_race_conditions=False, disable_frame_to_traceback=False)

    X = nc.dram_tensor("x_own", [HALF, D], F32, kind="ExternalInput").ap()
    UVWT = nc.dram_tensor("uvwT", [D, 4224], F32, kind="ExternalInput").ap()
    OWT = nc.dram_tensor("owT", [E, D], F32, kind="ExternalInput").ap()
    EXPERTS = nc.dram_tensor("experts", [8, D, D], F32, kind="ExternalInput").ap()
    GATEW = nc.dram_tensor("gateW", [D, 8], F32, kind="ExternalInput").ap()
    GB4 = nc.dram_tensor("gb4", [S, 4], F32, kind="ExternalInput").ap()
    GSC = nc.dram_tensor("gsc", [1, 2], F32, kind="ExternalInput").ap()
    RESS = nc.dram_tensor("res_scale", [1, D], F32, kind="ExternalInput").ap()
    W2 = nc.dram_tensor("w2", [1, 4608], F32, kind="ExternalInput").ap()
    COST = nc.dram_tensor("cosT", [64, HALF], F32, kind="ExternalInput").ap()
    SINT = nc.dram_tensor("sinT", [64, HALF], F32, kind="ExternalInput").ap()
    IDS = nc.dram_tensor("ids_lo", [128, NTC], I32, kind="ExternalInput").ap()
    TAGS = nc.dram_tensor("tags", [1, 6], I32, kind="ExternalInput").ap()
    TRI = nc.dram_tensor("tri", [128, 128], F32, kind="ExternalInput").ap()
    IDEN = nc.dram_tensor("ident", [128, 128], F32, kind="ExternalInput").ap()
    Y = nc.dram_tensor("y_own", [HALF, D], F32, kind="ExternalOutput").ap()

    SHV = nc.dram_tensor("shv", [L, E], BF16, kind="Internal", addr_space="Shared").ap()
    SHK = nc.dram_tensor("shk", [2, 128, HALF], BF16, kind="Internal", addr_space="Shared").ap()
    SHL = nc.dram_tensor("shl", [L, 8], F32, kind="Internal", addr_space="Shared").ap()
    FLG = nc.dram_tensor("flg", [2, 8], I32, kind="Internal", addr_space="Shared").ap()

    UT = nc.dram_tensor("uT", [E, HALF], BF16, kind="Internal").ap()
    X2D = nc.dram_tensor("x2d", [HALF, D], F32, kind="Internal").ap()
    XL2D = nc.dram_tensor("xl2d", [HALF, D], BF16, kind="Internal").ap()
    GBUF = nc.dram_tensor("gbuf", [4096, 1028], BF16, kind="Internal").ap()
    OBUF = nc.dram_tensor("obuf", [4096, D], F32, kind="Internal").ap()

    es = contextlib.ExitStack()
    sb = lambda n, shp, dt: es.enter_context(nc.sbuf_tensor(n, shp, dt))
    psal = lambda n, shp: es.enter_context(nc.psum_tensor(n, shp, F32))

    qT = sb("qT", [128, HALF], BF16)
    kTt = sb("kTt", [128, L], BF16)
    xt = [sb(f"xt{i}", [128, D], F32) for i in range(2)]
    evac = sb("evacs", [128, D], F32)
    scr = sb("scr", [128, QB], F32)
    scr2 = sb("scr2", [128, QB], F32)
    stats = sb("stats", [128, 8], F32)
    mcol = sb("mcol", [128, NTC], F32)
    rcol = sb("rcol", [128, NTC], F32)
    scol = sb("scol", [128, NTC], F32)
    m2col = sb("m2col", [128, NTC], F32)
    r2col = sb("r2col", [128, NTC], F32)
    gb4s = sb("gb4s", [128, 4], F32)
    gscs = sb("gscs", [1, 2], F32)
    ressr = sb("ressr", [1, D], F32)
    resb = sb("resb", [128, D], F32)
    idenb = sb("idenb", [128, 128], F32)
    tris = sb("tris", [128, 128], F32)
    ones1 = sb("ones1", [1, 128], F32)
    onesc = sb("onesc", [128, 1], F32)
    cbias = sb("cbias", [128, 1], F32)
    epsb = sb("epsb", [128, 1], F32)
    cm05 = sb("cm05", [128, 1], F32)
    cisq = sb("cisq", [128, 1], F32)
    gb128 = sb("gb128", [128, 1], F32)
    w2one = sb("w2one", [1, 1], F32)
    tags = sb("tagss", [1, 6], I32)
    tailb = sb("tailb", [128, 16], BF16)
    fbuf = sb("fbuf", [1, 8], I32)
    idss = sb("idss", [128, NTC], I32)
    idbf = sb("idbf", [128, 2 * NTC], BF16)
    zzf = sb("zzf", [128, D], F32)
    padp = sb("padp", [128, 4], BF16)
    bandt = sb("bandt", [128, QB], F32)
    gatewb = sb("gatewb", [128, 64], BF16)
    tlog = sb("tlog", [128, 256], F32)
    texp = sb("texp", [128, 256], F32)
    red32 = sb("red32", [128, 32], F32)
    red8 = sb("red8", [128, 8], F32)
    row8 = sb("row8", [1, 8], F32)
    lob = sb("lob", [1, 8], F32)
    hib = sb("hib", [1, 8], F32)
    midb = sb("midb", [1, 8], F32)
    t8a = sb("t8a", [1, 8], F32)
    kfrac = sb("kfrac", [1, 120], F32)
    candr = sb("candr", [1, 120], F32)
    rowK = sb("rowK", [1, 120], F32)
    gK = sb("gK", [1, 120], F32)
    cK2 = sb("cK2", [1, 120], F32)
    mid128 = sb("mid128", [128, 8], F32)
    gown = sb("gown", [128, 128], F32)
    selo = sb("selo", [128, 128], F32)
    hrdo = sb("hrdo", [128, 128], F32)
    wino = sb("wino", [128, 128], F32)
    accm = sb("accm", [128, 16], F32)
    cum = sb("cum", [128, 16], F32)
    tmp16 = sb("tmp16", [128, 16], F32)
    slots = sb("slots", [128, NTC], F32)
    sloti = sb("sloti", [128, NTC], I32)
    hwcol = sb("hwcol", [128, NTC], F32)
    cnts = sb("cnts", [128, 8], F32)
    pref = sb("pref", [128, 8], F32)
    spay = sb("spay", [128, 1028], BF16)
    xl2r = [sb(f"xl2r{i}", [128, D], BF16) for i in range(3)]
    xlt = sb("xlt", [128, D], BF16)
    idb16 = sb("idb16", [128, 128], BF16)
    vstage = [sb(f"vstage{i}", [128, 256], BF16) for i in range(4)]
    ustage = [sb(f"ustage{i}", [128, QB], BF16) for i in range(2)]
    wpan = [sb(f"wpan{i}", [128, 8 * 128], BF16) for i in range(2)]

    esA2 = contextlib.ExitStack()
    sbA2 = lambda n, shp, dt: esA2.enter_context(nc.sbuf_tensor(n, shp, dt))
    basT = sbA2("basT", [128, HALF], F32)
    cosS = sbA2("cosS", [64, HALF], F32)
    sinS = sbA2("sinS", [64, HALF], F32)
    xnf = sbA2("xnf", [128, D], F32)
    esA1 = contextlib.ExitStack()
    sbA1 = lambda n, shp, dt: esA1.enter_context(nc.sbuf_tensor(n, shp, dt))
    xnT = sbA1("xnT", [128, 8 * HALF], BF16)
    wvres = sbA1("wvres", [128, 8 * E], BF16)
    esA3 = contextlib.ExitStack()
    rlo = esA3.enter_context(nc.sbuf_tensor("rlo", [64, HALF], F32))
    rhi = esA3.enter_context(nc.sbuf_tensor("rhi", [64, HALF], F32))
    rql = esA3.enter_context(nc.sbuf_tensor("rql", [64, HALF], F32))
    rqh = esA3.enter_context(nc.sbuf_tensor("rqh", [64, HALF], F32))

    gB = {}
    gC = {}

    pA = psal("pA", [128, 1024])
    pB = psal("pB", [128, 1024])
    pC = psal("pC", [128, 1024])
    pD = psal("pD", [128, 1024])

    sems = {}
    for s in ("sQ", "sR", "sG", "sP", "sV", "sA", "sF"):
        sems[s] = es.enter_context(nc.semaphore(s))

    state = {}

    def program(px):
        SY, GP, V, T, SC = (px[e] for e in ENGS)
        cnt = SY.c

        if GP.h is not None:
            pidv = nc.partition_id(engines=[ET.Pool])
            hpar = pidv % 2
            kreg = nc.alloc_register(ET.Pool, "pollk")
            GP.h.reg_mov(kreg, 0)
            hreg = nc.alloc_register(ET.Pool, "hreg")
            GP.h.reg_alu(hreg, pidv, 2, ALU.mod)
            bcreg = nc.alloc_register(ET.Pool, "bcreg")
            GP.h.reg_mov(bcreg, 4095)
        else:
            hpar = None
            bcreg = 4095

        def dump_and_end(tiles):
            SY.wait("sV"); SY.wait("sA"); SY.wait("sP"); SY.wait("sG"); SY.wait("sR")
            for i, ap in enumerate(tiles[:NTC]):
                w = ap.shape[-1]
                SY.dma("sQ", Y[i * 128:(i + 1) * 128, 0:w], ap)
            GP.wait("sQ")
            SY.wait("sQ")

        def par_dma(sem, dst01, src_ap, src01=None):
            cnt[sem] = cnt.get(sem, 0) + 16
            if GP.h is None:
                return
            with GP.h.If_eq(hreg, 0):
                GP.h.dma_start(out=dst01[0], in_=src_ap if src01 is None else src01[0]).then_inc(sems[sem], 16)
            with GP.h.Else():
                GP.h.dma_start(out=dst01[1], in_=src_ap if src01 is None else src01[1]).then_inc(sems[sem], 16)

        def allwait(*targets):
            for eng in targets:
                for sm in ("sQ", "sR", "sG", "sP", "sV", "sA"):
                    px[eng].wait(sm)

        def sym_dst(tensor, off_expr, ap):
            return APc(tensor, off_expr, ap)

        SY.dma("sQ", gb4s[:, :], GB4[:, :])
        SY.dma("sQ", gscs[:, :], GSC[:, :])
        SY.dma("sQ", ressr[:, :], RESS[:, :])
        SY.dma("sQ", w2one[:, :], W2[0:1, 4095:4096])
        SY.dma("sQ", cosS[:, :], COST[:, :])
        SY.dma("sQ", sinS[:, :], SINT[:, :])
        SY.dma("sQ", idenb[:, :], IDEN[:, :])
        SY.dma("sQ", tris[:, :], TRI[:, :])
        SY.dma("sQ", tags[:, :], TAGS[:, :])
        SY.dma("sQ", idss[:, :], IDS[:, :])
        SY.dma("sQ", stats.ap().bitcast(I32)[:, 4:5], idss[:, NTC - 1:NTC])
        GP.dma("sG", rr(gatewb.ap(), "p (a e) -> p a e", a=8), rr(GATEW, "(a p) e -> p a e", p=128))
        V.op("sV", "memset", epsb[:, :], EPS1)
        V.op("sV", "memset", cm05[:, :], -0.5)
        V.op("sV", "memset", cisq[:, :], INV_SQRT_S)
        V.op("sV", "memset", ones1[:, :], 1.0)
        V.op("sV", "memset", onesc[:, :], 1.0)
        V.op("sV", "memset", zzf[:, :], 0.0)
        V.op("sV", "memset", padp[:, :], PAD_BF)
        V.op("sV", "memset", spay[:, 1026:1028], 0.0)
        for k in range(15):
            V.op(None, "memset", kfrac[:, k * 8:(k + 1) * 8], (k + 1) / 16.0)
        V.wait("sQ")
        V.op(None, "tensor_copy", out=idb16[:, :], in_=idenb[:, :])
        V.op("sV", "tensor_copy", out=idbf[:, :], in_=idss.ap().bitcast(BF16)[:, :])
        T.wait("sQ")
        T.wait("sV")
        T.op(None, "matmul", pC[:, 0:512], ones1[0:1, :], ressr[0:1, 0:512], start=True, stop=True)
        T.op(None, "matmul", pC[:, 512:1024], ones1[0:1, :], ressr[0:1, 512:1024], start=True, stop=True)
        T.op(None, "matmul", pD[:, 0:1], ones1[0:1, :], w2one[0:1, 0:1], start=True, stop=True)
        T.op("sP", "matmul", pD[:, 1:2], ones1[0:1, :], gscs[0:1, 0:1], start=True, stop=True)
        T.fence("sP")
        V.wait("sP")
        V.op(None, "tensor_copy", out=resb[:, :], in_=pC[:, :])
        V.op(None, "tensor_copy", out=gb128[:, :], in_=pD[:, 1:2])
        V.op("sV", "tensor_scalar_mul", out=cbias[:, :], in0=pD[:, 0:1], scalar1=INV_SQRT_S)
        V.fence("sV")
        v_prolog = cnt["sV"]

        sum_ring = []
        for tc in range(NTC):
            buf = xt[tc % 2]
            if tc >= 2:
                SY.wait("sV", sum_ring[tc - 2])
            SY.dma("sQ", buf[:, :], X[tc * 128:(tc + 1) * 128, :])
            SY.dma("sQ", stats[:, 4:8], buf[:, 1020:1024])
            V.wait("sQ")
            SC.wait("sQ")
            if tc >= 1:
                SC.wait("sV")
            SC.op("sA", "activation", out=scr[:, 0:1024], in_=buf[:, :], func=AF.Square)
            SC.fence("sA")
            V.op(None, "reduce_sum", out=mcol[:, tc:tc + 1], in_=buf[:, :], axis=AX.X)
            V.wait("sA")
            V.op("sV", "reduce_sum", out=rcol[:, tc:tc + 1], in_=scr[:, 0:1024], axis=AX.X)
            sum_ring.append(cnt["sV"])
        V.op(None, "tensor_scalar_mul", out=mcol[:, :], in0=mcol[:, :], scalar1=1.0 / D)
        V.op(None, "tensor_scalar_mul", out=rcol[:, :], in0=rcol[:, :], scalar1=1.0 / D)
        V.op(None, "tensor_mul", out=cum[:, :], in0=mcol[:, :], in1=mcol[:, :])
        V.op("sV", "tensor_sub", out=rcol[:, :], in0=rcol[:, :], in1=cum[:, :])
        V.fence("sV")
        SC.wait("sV")
        SC.op(None, "activation", out=tmp16[:, :], in_=rcol[:, :], func=AF.Ln, bias=epsb[:, :])
        SC.op("sA", "activation", out=cum[:, :], in_=rcol[:, :], func=AF.Sqrt)
        SC.wait("sA")
        SC.op("sA", "activation", out=rcol[:, :], in_=tmp16[:, :], func=AF.Exp, scale=cm05[:, :])
        SC.fence("sA")
        V.wait("sA")
        V.op(None, "tensor_mul", out=cum[:, :], in0=cum[:, :], in1=rcol[:, :])
        V.op(None, "tensor_scalar_max", out=cum[:, :], in0=cum[:, :], scalar1=EPS1)
        V.op("sV", "reciprocal", out=cum[:, :], in_=cum[:, :])
        V.fence("sV")
        SC.wait("sV")
        SC.op("sA", "mul", out=cum[:, :], in_=cum[:, :], mul=gb128[:, :])
        SC.fence("sA")
        V.wait("sA")
        V.op("sV", "tensor_mul", out=scol[:, :], in0=cum[:, :], in1=rcol[:, :])
        V.fence("sV")
        ln1_ring = []
        for tc in range(NTC):
            buf = xt[tc % 2]
            if tc >= 2:
                SY.wait("sV", ln1_ring[tc - 2])
            else:
                SY.wait("sV", sum_ring[NTC - 2 + tc])
            SY.dma("sQ", buf[:, :], X[tc * 128:(tc + 1) * 128, :])
            SY.dma("sQ", stats[:, 4:8], buf[:, 1020:1024])
            V.wait("sQ")
            V.op("sV", "tensor_scalar", out=xnf[:, :], in0=buf[:, :], scalar1=mcol[:, tc:tc + 1],
                 scalar2=scol[:, tc:tc + 1], op0=ALU.subtract, op1=ALU.mult)
            V.fence("sV")
            ln1_ring.append(cnt["sV"])
            T.wait("sV")
            for dc in range(8):
                T.op("sP" if dc == 7 else None, "transpose",
                     rr(pA.ap(), "p (a b) -> p a b", a=8)[:, dc, :],
                     xnf[:, dc * 128:(dc + 1) * 128], idenb[:, :])
            T.fence("sP")
            V.wait("sP")
            V.op("sV", "tensor_copy",
                 out=rr(xnT.ap(), "p (a b) -> p a b", a=8)[:, :, tc * 128:(tc + 1) * 128],
                 in_=rr(pA.ap(), "p (a b) -> p a b", a=8)[:, :, :])
            V.fence("sV")
            T.wait("sV")
        for dc in range(8):
            GP.dma("sG", wvres[:, dc * E:(dc + 1) * E], UVWT[dc * 128:(dc + 1) * 128, E:2 * E])
        g_wv = cnt["sG"]

        uv_pring, uv_aring, uv_qring = [], [], []
        for cc in range(17):
            col0 = cc * 128 if cc < 16 else 4096
            pa, pb = (pA, pB) if cc % 2 == 0 else (pC, pD)
            wbuf = wpan[cc % 2]
            if cc >= 2:
                GP.wait("sP", uv_pring[cc - 2])
            GP.dma("sG", rr(wbuf[:, :], "p (a c) -> p a c", a=8), rr(UVWT[:, col0:col0 + 128], "(a p) c -> p a c", p=128))
            GP.dma("sG", tailb[:, 4:8], wbuf[:, 8 * 128 - 4:8 * 128])
            T.wait("sG")
            if cc >= 2:
                T.wait("sA", uv_aring[cc - 2])
            for dc in range(8):
                st, sp = dc == 0, dc == 7
                wsl = wbuf[:, dc * 128:(dc + 1) * 128]
                for half in range(2):
                    P = pa if half == 0 else pb
                    for nn2 in range(2):
                        t0 = half * 1024 + nn2 * 512
                        T.op("sP" if (half == 1 and nn2 == 1) else None, "matmul",
                             P[:, nn2 * 512:nn2 * 512 + 512], wsl,
                             xnT[:, dc * HALF + t0:dc * HALF + t0 + 512], start=st, stop=sp)
            uv_pring.append(cnt["sP"])
            if cc == 16:
                T.fence("sP")
                SC.wait("sP")
            else:
                SC.wait("sP", uv_pring[-1] + 1)
            if cc < 16:
                if cc >= 1:
                    SC.wait("sQ", uv_qring[cc - 1])
                SC.op(None, "activation", out=evac.ap().bitcast(BF16)[:, 0:1024], in_=pa[:, :], func=AF.Silu)
                SC.op("sA", "activation", out=evac.ap().bitcast(BF16)[:, 1024:2048], in_=pb[:, :], func=AF.Silu)
                SC.fence("sA")
                uv_aring.append(cnt["sA"])
                SY.wait("sA")
                SY.dma("sQ", UT[cc * 128:(cc + 1) * 128, :], evac.ap().bitcast(BF16)[:, 0:2048])
                uv_qring.append(cnt["sQ"])
            else:
                SC.op(None, "activation", out=basT[:, 0:1024], in_=pa[:, :], func=AF.Silu)
                SC.op("sA", "activation", out=basT[:, 1024:2048], in_=pb[:, :], func=AF.Silu)
                SC.fence("sA")
                uv_aring.append(cnt["sA"])
        a_base = cnt["sA"]

        if _UPTO == "p2a":
            SY.wait("sA", a_base)
            dump_and_end([basT[:, 0:1024], basT[:, 1024:2048],
                          xnT.ap().bitcast(F32)[:, 0:1024], xnT.ap().bitcast(F32)[:, 1024:2048]])
            return
        V.wait("sA", a_base)
        for br in range(2):
            g0, b0 = (0, 1) if br == 0 else (2, 3)
            V.wait("sQ")
            V.op(None, "tensor_scalar", out=scr[:, :], in0=basT[:, 0:QB], scalar1=gb4s[:, g0:g0 + 1],
                 scalar2=gb4s[:, b0:b0 + 1], op0=ALU.mult, op1=ALU.add)
            V.op("sV", "tensor_scalar", out=scr2[:, :], in0=basT[:, QB:2 * QB], scalar1=gb4s[:, g0:g0 + 1],
                 scalar2=gb4s[:, b0:b0 + 1], op0=ALU.mult, op1=ALU.add)
            V.fence("sV")
            SY.wait("sV")
            SY.dma("sQ", rlo[:, 0:QB], scr[0:64, :])
            SY.dma("sQ", rlo[:, QB:2 * QB], scr2[0:64, :])
            SY.dma("sQ", rhi[:, 0:QB], scr[64:128, :])
            SY.dma("sQ", rhi[:, QB:2 * QB], scr2[64:128, :])
            SY.dma("sQ", stats[0:64, 4:8], rhi[:, 2 * QB - 4:2 * QB])
            V.wait("sQ")
            V.op(None, "tensor_mul", out=rql[:, :], in0=rlo[:, :], in1=cosS[:, :])
            V.op(None, "tensor_mul", out=rqh[:, :], in0=rhi[:, :], in1=sinS[:, :])
            V.op(None, "tensor_sub", out=rql[:, :], in0=rql[:, :], in1=rqh[:, :])
            V.op(None, "tensor_mul", out=rqh[:, :], in0=rhi[:, :], in1=cosS[:, :])
            V.op(None, "tensor_mul", out=rlo[:, :], in0=rlo[:, :], in1=sinS[:, :])
            V.op(None, "tensor_add", out=rqh[:, :], in0=rqh[:, :], in1=rlo[:, :])
            V.op(None, "tensor_copy", out=scr.ap().bitcast(BF16)[0:64, 0:HALF], in_=rql[:, :])
            V.op("sV", "tensor_copy", out=scr2.ap().bitcast(BF16)[0:64, 0:HALF], in_=rqh[:, :])
            V.fence("sV")
            SY.wait("sV")
            if br == 0:
                SY.dma("sQ", qT[0:64, :], scr.ap().bitcast(BF16)[0:64, 0:HALF])
                SY.dma("sQ", qT[64:128, :], scr2.ap().bitcast(BF16)[0:64, 0:HALF])
                SY.dma("sQ", tailb[0:64, 8:12], qT[64:128, HALF - 4:HALF])
            else:
                SY.dma("sQ", kTt[0:64, 0:HALF], scr.ap().bitcast(BF16)[0:64, 0:HALF])
                SY.dma("sQ", kTt[64:128, 0:HALF], scr2.ap().bitcast(BF16)[0:64, 0:HALF])
                SY.dma("sQ", tailb[0:64, 8:12], kTt[64:128, HALF - 4:HALF])
        GP.wait("sQ")
        par_dma("sG", (APc(SHK.tensor, 0, [[HALF, 128], [1, HALF]]),
                       APc(SHK.tensor, 128 * HALF, [[HALF, 128], [1, HALF]])),
                kTt[:, 0:HALF])
        GP.wait("sG")
        par_dma("sG", (APc(FLG.tensor, 0, [[8, 1], [1, 2]]),
                       APc(FLG.tensor, 8, [[8, 1], [1, 2]])),
                tags[:, 0:2])
        v_pring, v_aring, v_gring = [], [], []
        T.wait("sG", g_wv)
        T.wait("sA", a_base)
        for tc in range(NTC):
            pa, pb = (pA, pB) if tc % 2 == 0 else (pC, pD)
            if tc >= 2:
                T.wait("sA", v_aring[tc - 2])
            for dc in range(8):
                st, sp = dc == 0, dc == 7
                lhs = xnT[:, dc * HALF + tc * 128:dc * HALF + (tc + 1) * 128]
                for half in range(2):
                    P = pa if half == 0 else pb
                    for nn2 in range(2):
                        e0 = half * 1024 + nn2 * 512
                        T.op("sP" if (half == 1 and nn2 == 1) else None, "matmul",
                             P[:, nn2 * 512:nn2 * 512 + 512], lhs,
                             wvres[:, dc * E + e0:dc * E + e0 + 512], start=st, stop=sp)
            v_pring.append(cnt["sP"])
            if tc == NTC - 1:
                T.fence("sP")
                SC.wait("sP")
            else:
                SC.wait("sP", v_pring[-1] + 1)
            if tc >= 1:
                SC.wait("sG", v_gring[tc - 1])
            SC.op(None, "activation", out=evac.ap().bitcast(BF16)[:, 0:1024], in_=pa[:, :], func=AF.Silu)
            SC.op("sA", "activation", out=evac.ap().bitcast(BF16)[:, 1024:2048], in_=pb[:, :], func=AF.Silu)
            SC.fence("sA")
            v_aring.append(cnt["sA"])
            GP.wait("sA")
            par_dma("sG", (SHV[tc * 128:(tc + 1) * 128, :],
                           SHV[HALF + tc * 128:HALF + (tc + 1) * 128, :]),
                    evac.ap().bitcast(BF16)[:, 0:2048])
            v_gring.append(cnt["sG"])

        GP.wait("sG")
        par_dma("sG", (APc(FLG.tensor, 2, [[8, 1], [1, 2]]),
                       APc(FLG.tensor, 10, [[8, 1], [1, 2]])),
                tags[:, 2:4])

        if not gB:
            esA3.close()
            esA1.close()
            esA2.close()
            esB = contextlib.ExitStack()
            gB["kerT"] = esB.enter_context(nc.sbuf_tensor("kerT", [128, 32 * QB], BF16))
            gB["uaq"] = esB.enter_context(nc.sbuf_tensor("uaq", [128, 16 * QB], BF16))
            gB["owts"] = esB.enter_context(nc.sbuf_tensor("owts", [128, 16 * 1024], BF16))
            gB["_es"] = esB
        kerT, uaq, owts = gB["kerT"], gB["uaq"], gB["owts"]

        for ecc in range(16):
            GP.dma("sG", owts[:, ecc * 1024:(ecc + 1) * 1024], OWT[ecc * 128:(ecc + 1) * 128, :])
        g_ow = cnt["sG"]
        if GP.h is not None:
            psrcK = sym_dst(FLG.tensor, (1 - hpar) * 8, [[8, 1], [1, 2]])
            expK = GP.h.value_load(tags.ap()[0:1, 0:1])

            def pollK():
                GP.h.reg_add(kreg, kreg, 16)
                GP.h.dma_start(out=fbuf[:, 0:2], in_=psrcK).then_inc(sems["sF"], 16)
                GP.h.wait_ge(sems["sF"], kreg)
                val = GP.h.value_load(fbuf.ap()[0:1, 0:1])
                return (val - expK) != 0

            with GP.h.While(pollK):
                GP.h.nop()
        GP.dma("sG", kTt[:, 0:HALF], SHK[0, :, :])
        GP.dma("sG", kTt[:, HALF:L], SHK[1, :, :])
        GP.dma("sG", tailb[:, 4:8], kTt[:, L - 4:L])
        g_kt = cnt["sG"]
        if GP.h is not None:
            psrcV = sym_dst(FLG.tensor, (1 - hpar) * 8 + 2, [[8, 1], [1, 2]])
            expV = GP.h.value_load(tags.ap()[0:1, 2:3])

            def pollV():
                GP.h.reg_add(kreg, kreg, 16)
                GP.h.dma_start(out=fbuf[:, 2:4], in_=psrcV).then_inc(sems["sF"], 16)
                GP.h.wait_ge(sems["sF"], kreg)
                val = GP.h.value_load(fbuf.ap()[0:1, 2:3])
                return (val - expV) != 0

            with GP.h.While(pollV):
                GP.h.nop()
        GP.dma("sG", fbuf[:, 4:6], APc(FLG.tensor, 4, [[8, 1], [1, 2]]))
        g_vok = cnt["sG"]
        GP.wait("sV", v_prolog)
        for i in range(32):
            GP.dma("sG", GBUF[i * 128:(i + 1) * 128, 0:1024], zzf.ap().bitcast(BF16)[:, 0:1024])
            GP.dma("sG", GBUF[i * 128:(i + 1) * 128, 1024:1028], padp[:, :])

        shl_ring = []
        xq_ring = []
        xls_v, tp_p, xlt_a, gate_p, scrc_v = {}, {}, {}, {}, {}
        for qb in range(2):
            q0 = qb * QB
            if qb == 0:
                T.wait("sG", g_kt)
                T.wait("sQ")
                SC.wait("sV")
            else:
                SC.wait("sP")
                SC.wait("sV")
                T.wait("sV")
                T.wait("sA")
                V.wait("sG", shl_ring[-1])
            qk_aring = []
            qk_pring = []
            for kc in range(32):
                P = pA if kc % 2 == 0 else pB
                if kc >= 2:
                    T.wait("sA", qk_aring[kc - 2])
                T.op(None, "matmul", P[:, 0:512], kTt[:, kc * 128:(kc + 1) * 128],
                     qT[:, q0:q0 + 512], start=True, stop=True)
                T.op("sP", "matmul", P[:, 512:1024], kTt[:, kc * 128:(kc + 1) * 128],
                     qT[:, q0 + 512:q0 + 1024], start=True, stop=True)
                qk_pring.append(cnt["sP"])
                if kc == 31:
                    T.fence("sP")
                pwait = qk_pring[kc] + 1 if kc < 31 else cnt["sP"]
                if kc in BAND_KC[qb]:
                    A_dev = 1024 + kc * 128 - qb * 1024
                    SY.wait("sV")
                    SY.dma("sQ", bandt[:, :], APc(W2.tensor, A_dev, [[1, 128], [1, QB]]))
                    SY.dma("sQ", stats[:, 4:8], bandt[:, 1020:1024])
                    V.wait("sQ")
                    V.wait("sP", pwait)
                    if len(qk_aring) >= 1:
                        V.wait("sA", qk_aring[-1])
                    bst = bandt.ap().ap[0][0]
                    revv = APc(bandt.ap().tensor, 1023, [[bst, 128], [-1, QB]])
                    V.op("sV", "tensor_add", out=scr[:, :], in0=P[:, :], in1=revv)
                    V.fence("sV")
                    SC.wait("sV")
                    SC.op("sA", "activation", out=scr2[:, :], in_=scr[:, :], func=AF.Relu,
                          scale=cisq[:, :])
                else:
                    SC.wait("sP", pwait)
                    SC.op("sA", "activation", out=scr2[:, :], in_=P[:, :], func=AF.Relu,
                          scale=cisq[:, :], bias=cbias[:, :])
                SC.wait("sA")
                SC.op("sA", "activation", out=kerT[:, kc * QB:(kc + 1) * QB], in_=scr2[:, :],
                      func=AF.Square)
                qk_aring.append(cnt["sA"])
            SC.fence("sA")
            av_vring = []
            vkc_ring = []
            for g in range(8):
                Pg = (pA, pB) if g % 2 == 0 else (pC, pD)
                if g >= 1:
                    SY.wait("sV", av_vring[g - 1])
                for sub in range(2):
                    ecc = g * 2 + sub
                    SY.dma("sQ", ustage[sub][:, :], UT[ecc * 128:(ecc + 1) * 128, q0:q0 + QB])
                if g == 0 and qb == 0:
                    SC.wait("sG", g_vok)
                if g == 0:
                    T.wait("sA")
                if g >= 2:
                    T.wait("sV", av_vring[g - 2])
                vload_cnt = []
                def vload(kcl, gg=g):
                    if vkc_ring:
                        SC.wait("sP", vkc_ring[-1])
                    SC.dma("sR", vstage[kcl % 4][:, :],
                           SHV[kcl * 128:(kcl + 1) * 128, gg * 256:(gg + 1) * 256])
                    vload_cnt.append(cnt["sR"])
                for kcl in range(3):
                    vload(kcl)
                for kc in range(32):
                    vs = vstage[kc % 4]
                    if kc + 3 < 32:
                        vload(kc + 3)
                        if kc + 3 == 31:
                            SC.dma("sR", tailb[:, 0:4], vstage[31 % 4][:, 252:256])
                    T.wait("sR", vload_cnt[kc + 1] if kc + 1 < 32 else cnt["sR"])
                    for sub in range(2):
                        P = Pg[sub]
                        st, sp = kc == 0, kc == 31
                        T.op(None, "matmul", P[:, 0:512], vs[:, sub * 128:(sub + 1) * 128],
                             kerT[:, kc * QB:kc * QB + 512], start=st, stop=sp)
                        T.op("sP" if sub == 1 else None, "matmul",
                             P[:, 512:1024], vs[:, sub * 128:(sub + 1) * 128],
                             kerT[:, kc * QB + 512:kc * QB + 1024], start=st, stop=sp)
                    vkc_ring.append(cnt["sP"])
                if g == 7:
                    T.fence("sP")
                    V.wait("sP")
                else:
                    V.wait("sP", vkc_ring[-1] + 1)
                V.wait("sQ")
                for sub in range(2):
                    ecc = g * 2 + sub
                    V.op("sV" if sub == 1 else None, "tensor_mul",
                         out=uaq[:, ecc * QB:(ecc + 1) * QB],
                         in0=Pg[sub][:, :], in1=ustage[sub][:, :])
                V.fence("sV")
                av_vring.append(cnt["sV"])
            if _UPTO == "av" and qb == 0:
                V.wait("sV")
                tiles = [uaq.ap().bitcast(F32)[:, i * 1024:(i + 1) * 1024] for i in range(8)]
                tiles += [kerT.ap().bitcast(F32)[:, i * 1024:(i + 1) * 1024] for i in range(8)]
                dump_and_end(tiles)
                return
            op_vring = []
            T.wait("sV")
            if qb == 0:
                T.wait("sG", g_ow)

            def logits_pe(j, flush):
                xsl = xl2r[j % 3]
                T.wait("sV", xls_v[j])
                if j >= 1:
                    T.wait("sA", xlt_a[j - 1])
                for dc in range(8):
                    T.op("sP" if dc == 7 else None, "transpose",
                         rr(pC.ap().bitcast(BF16)[:, 0:1024], "p (a b) -> p a b", a=8)[:, dc, :],
                         xsl[:, dc * 128:(dc + 1) * 128], idb16[:, :])
                T.fence("sP")
                tp_p[j] = cnt["sP"]
                SC.wait("sP", tp_p[j])
                SC.op("sA", "activation", out=xlt[:, :], in_=pC.ap().bitcast(BF16)[:, 0:1024],
                      func=AF.Copy)
                SC.fence("sA")
                xlt_a[j] = cnt["sA"]
                T.wait("sA", xlt_a[j])
                if j >= 1:
                    T.wait("sV", scrc_v[j - 1])
                for dc in range(8):
                    T.op("sP" if dc == 7 else None, "matmul", pD[:, 0:8],
                         xlt[:, dc * 128:(dc + 1) * 128], gatewb[:, dc * 8:(dc + 1) * 8],
                         start=(dc == 0), stop=(dc == 7))
                gate_p[j] = cnt["sP"]
                if flush:
                    T.fence("sP")
                    gate_p[j] = cnt["sP"]

            def logits_v(j, flush):
                V.wait("sP", gate_p[j] if flush else gate_p[j] + 1)
                if shl_ring:
                    V.wait("sG", shl_ring[-1])
                V.op("sV", "tensor_copy", out=scr[:, 0:8], in_=pD[:, 0:8])
                V.fence("sV")
                scrc_v[j] = cnt["sV"]
                GP.wait("sV")
                par_dma("sG", (SHL[j * 128:(j + 1) * 128, :],
                               SHL[HALF + j * 128:HALF + (j + 1) * 128, :]),
                        scr[:, 0:8])
                shl_ring.append(cnt["sG"])

            for tcl in range(8):
                tc = qb * 8 + tcl
                P = pA if tcl % 2 == 0 else pB
                buf = xt[tcl % 2]
                if tcl >= 2:
                    SY.wait("sV", op_vring[tcl - 2])
                    T.wait("sV", op_vring[tcl - 2])
                SY.dma("sQ", buf[:, :], X[tc * 128:(tc + 1) * 128, :])
                for ecc in range(16):
                    st, sp = ecc == 0, ecc == 15
                    lhs = uaq[:, ecc * QB + tcl * 128:ecc * QB + (tcl + 1) * 128]
                    T.op(None, "matmul", P[:, 0:512], lhs,
                         owts[:, ecc * 1024:ecc * 1024 + 512], start=st, stop=sp)
                    T.op("sP" if sp else None, "matmul", P[:, 512:1024], lhs,
                         owts[:, ecc * 1024 + 512:(ecc + 1) * 1024], start=st, stop=sp)
                T.fence("sP")
                ecc_f = cnt["sP"]
                if tcl >= 2:
                    logits_pe(tc - 2, False)
                V.wait("sP", ecc_f)
                V.wait("sQ")
                if shl_ring:
                    V.wait("sG", shl_ring[-1])
                V.op(None, "tensor_scalar", out=scr[:, :], in0=buf[:, :],
                     scalar1=mcol[:, tc:tc + 1], scalar2=rcol[:, tc:tc + 1],
                     op0=ALU.subtract, op1=ALU.mult)
                V.op(None, "tensor_mul", out=scr[:, :], in0=scr[:, :], in1=resb[:, :])
                V.op(None, "tensor_add", out=scr[:, :], in0=scr[:, :], in1=buf[:, :])
                V.op("sV", "tensor_add", out=evac[:, 0:1024], in0=scr[:, :], in1=P[:, :])
                V.fence("sV")
                op_vring.append(cnt["sV"])
                SY.wait("sV")
                SY.dma("sQ", X2D[tc * 128:(tc + 1) * 128, :], evac[:, 0:1024])
                SC.wait("sV")
                SC.op("sA", "activation", out=scr2[:, :], in_=evac[:, 0:1024], func=AF.Square)
                SC.fence("sA")
                V.op(None, "reduce_sum", out=m2col[:, tc:tc + 1], in_=evac[:, 0:1024], axis=AX.X)
                V.wait("sA")
                V.op(None, "reduce_sum", out=r2col[:, tc:tc + 1], in_=scr2[:, :], axis=AX.X)
                V.op(None, "tensor_scalar_mul", out=m2col[:, tc:tc + 1], in0=m2col[:, tc:tc + 1],
                     scalar1=1.0 / D)
                V.op(None, "tensor_scalar_mul", out=r2col[:, tc:tc + 1], in0=r2col[:, tc:tc + 1],
                     scalar1=1.0 / D)
                V.op(None, "tensor_mul", out=cum[:, tc:tc + 1], in0=m2col[:, tc:tc + 1],
                     in1=m2col[:, tc:tc + 1])
                V.op("sV", "tensor_sub", out=r2col[:, tc:tc + 1], in0=r2col[:, tc:tc + 1],
                     in1=cum[:, tc:tc + 1])
                V.fence("sV")
                SC.wait("sV")
                SC.op(None, "activation", out=tmp16[:, tc:tc + 1], in_=r2col[:, tc:tc + 1],
                      func=AF.Ln, bias=epsb[:, :])
                SC.op("sA", "activation", out=r2col[:, tc:tc + 1], in_=tmp16[:, tc:tc + 1],
                      func=AF.Exp, scale=cm05[:, :])
                SC.fence("sA")
                V.wait("sA")
                if tc >= 3:
                    V.wait("sP", tp_p[tc - 3])
                    V.wait("sQ", xq_ring[tc - 3])
                V.op("sV", "tensor_scalar", out=xl2r[tc % 3][:, :], in0=evac[:, 0:1024],
                     scalar1=m2col[:, tc:tc + 1], scalar2=r2col[:, tc:tc + 1],
                     op0=ALU.subtract, op1=ALU.mult)
                V.fence("sV")
                xls_v[tc] = cnt["sV"]
                SY.wait("sV", xls_v[tc])
                SY.dma("sQ", XL2D[tc * 128:(tc + 1) * 128, :], xl2r[tc % 3][:, :])
                xq_ring.append(cnt["sQ"])
                if tcl >= 2:
                    logits_v(tc - 2, False)
            logits_pe(qb * 8 + 6, False)
            logits_v(qb * 8 + 6, False)
            logits_pe(qb * 8 + 7, True)
            logits_v(qb * 8 + 7, True)
        if _UPTO == "x2":
            for tc in range(NTC):
                bx = xt[tc % 2]
                SY.dma("sQ", bx[:, :], X2D[tc * 128:(tc + 1) * 128, :])
                SY.wait("sQ")
                SY.dma("sQ", Y[tc * 128:(tc + 1) * 128, :], bx[:, :])
                SY.wait("sQ")
            GP.wait("sQ")
            return
        GP.wait("sG")
        par_dma("sG", (APc(FLG.tensor, 4, [[8, 1], [1, 2]]),
                       APc(FLG.tensor, 12, [[8, 1], [1, 2]])),
                tags[:, 4:6])
        if GP.h is not None:
            psrcf2 = sym_dst(FLG.tensor, (1 - hpar) * 8 + 4, [[8, 1], [1, 2]])
            exp_tag2 = GP.h.value_load(tags.ap()[0:1, 4:5])

            def poll2():
                GP.h.reg_add(kreg, kreg, 16)
                GP.h.dma_start(out=fbuf[:, 2:4], in_=psrcf2).then_inc(sems["sF"], 16)
                GP.h.wait_ge(sems["sF"], kreg)
                val = GP.h.value_load(fbuf.ap()[0:1, 2:3])
                return (val - exp_tag2) != 0

            with GP.h.While(poll2):
                GP.h.nop()
        GP.dma("sG", rr(tlog.ap(), "p (c e) -> p c e", c=32), rr(SHL, "(c p) e -> p c e", p=128))
        GP.dma("sG", stats[:, 0:4], tlog[:, 252:256])
        V.wait("sG")
        V.op("sV", "tensor_scalar_max", out=tlog[:, :], in0=tlog[:, :], scalar1=1e-6)
        V.fence("sV")
        SC.wait("sV")
        SC.op("sA", "activation", out=tlog[:, :], in_=tlog[:, :], func=AF.Ln)
        SC.fence("sA")
        SC.wait("sA")
        tview = rr(tlog.ap(), "p (c e) -> p c e", c=32)
        for it in range(8):
            SC.wait("sV")
            SC.op("sA", "activation", out=texp[:, :], in_=tlog[:, :], func=AF.Exp)
            SC.fence("sA")
            V.wait("sA")
            V.op("sV", "tensor_reduce", out=red8[:, :],
                 in_=rr(texp.ap(), "p (c e) -> p e c", c=32), axis=AX.X, op=ALU.add)
            V.fence("sV")
            T.wait("sV")
            T.op("sP", "matmul", pD[0:1, 0:8], onesc[:, :], red8[:, :], start=True, stop=True)
            T.fence("sP")
            SC.wait("sP")
            SC.op("sA", "activation", out=row8[:, :], in_=pD[0:1, 0:8], func=AF.Ln)
            SC.fence("sA")
            T.wait("sA")
            T.op("sP", "matmul", pD[:, 8:16], ones1[0:1, :], row8[0:1, :], start=True, stop=True)
            T.fence("sP")
            V.wait("sP")
            V.op(None, "tensor_copy", out=mid128[:, :], in_=pD[:, 8:16])
            V.op("sV", "tensor_sub", out=tview, in0=tview,
                 in1=mid128.ap().unsqueeze(1).broadcast_to([128, 32, 8]))
            V.fence("sV")
            SC.wait("sV")
            SC.op("sA", "activation", out=texp[:, :], in_=tlog[:, :], func=AF.Exp)
            SC.fence("sA")
            V.wait("sA")
            V.op("sV", "tensor_reduce", out=red32[:, :], in_=tview if False else
                 rr(texp.ap(), "p (c e) -> p c e", c=32), axis=AX.X, op=ALU.add)
            V.fence("sV")
            SC.wait("sV")
            SC.op("sA", "activation", out=red32[:, :], in_=red32[:, :], func=AF.Ln)
            SC.fence("sA")
            V.wait("sA")
            V.op("sV", "tensor_sub", out=tview, in0=tview,
                 in1=red32.ap().unsqueeze(2).broadcast_to([128, 32, 8]))
            V.fence("sV")
        SC.wait("sV")
        SC.op("sA", "activation", out=texp[:, :], in_=tlog[:, :], func=AF.Exp)
        SC.fence("sA")
        V.wait("sA")
        V.op(None, "memset", lob[:, :], 0.0)
        V.op("sV", "memset", hib[:, :], 1.0)
        kfr3 = rr(kfrac.ap(), "p (k e) -> p k e", k=15)
        cnd3 = rr(candr.ap(), "p (k e) -> p k e", k=15)
        v8 = lambda ap: ap.unsqueeze(1).broadcast_to([1, 15, 8])
        for rnd in range(7):
            V.op(None, "tensor_sub", out=t8a[:, :], in0=hib[:, :], in1=lob[:, :])
            V.op(None, "tensor_tensor", out=cnd3, in0=kfr3, in1=v8(t8a.ap()), op=ALU.mult)
            V.op("sV", "tensor_tensor", out=cnd3, in0=cnd3, in1=v8(lob.ap()), op=ALU.add)
            V.fence("sV")
            T.wait("sV")
            T.op("sP", "matmul", pD[:, 0:120], ones1[0:1, :], candr[0:1, :], start=True, stop=True)
            T.fence("sP")
            V.wait("sP")
            for k in range(15):
                V.op(None, "tensor_tensor", out=rr(tlog.ap(), "p (c e) -> p c e", c=32),
                     in0=rr(texp.ap(), "p (c e) -> p c e", c=32),
                     in1=pD[:, k * 8:(k + 1) * 8].unsqueeze(1).broadcast_to([128, 32, 8]),
                     op=ALU.is_gt)
                V.op("sV" if k == 14 else None, "tensor_reduce", out=scr2[:, k * 8:(k + 1) * 8],
                     in_=rr(tlog.ap(), "p (c e) -> p e c", c=32), axis=AX.X, op=ALU.add)
            V.fence("sV")
            T.wait("sV")
            T.op("sP", "matmul", pD[0:1, 0:120], onesc[:, :], scr2[:, 0:120], start=True, stop=True)
            T.fence("sP")
            V.wait("sP")
            V.op(None, "tensor_copy", out=rowK[:, :], in_=pD[0:1, 0:120])
            V.op(None, "tensor_single_scalar", out=gK[:, :], in_=rowK[:, :], scalar=511.5, op=ALU.is_ge)
            V.op(None, "tensor_mul", out=rowK[:, :], in0=candr[:, :], in1=gK[:, :])
            V.op(None, "tensor_reduce", out=t8a[:, :],
                 in_=rr(rowK.ap(), "p (k e) -> p e k", k=15), axis=AX.X, op=ALU.max)
            V.op(None, "tensor_tensor", out=lob[:, :], in0=lob[:, :], in1=t8a[:, :], op=ALU.max)
            V.op(None, "tensor_scalar_mul", out=cK2[:, :], in0=gK[:, :], scalar1=2.0)
            V.op(None, "tensor_tensor", out=cK2[:, :], in0=cK2[:, :], in1=candr[:, :], op=ALU.add)
            V.op(None, "tensor_tensor", out=rowK[:, :], in0=cK2[:, :], in1=gK[:, :], op=ALU.mult)
            V.op(None, "tensor_sub", out=cK2[:, :], in0=cK2[:, :], in1=rowK[:, :])
            V.op(None, "tensor_scalar_mul", out=rowK[:, :], in0=gK[:, :], scalar1=2.0)
            V.op(None, "tensor_add", out=cK2[:, :], in0=cK2[:, :], in1=rowK[:, :])
            V.op(None, "tensor_reduce", out=t8a[:, :],
                 in_=rr(cK2.ap(), "p (k e) -> p e k", k=15), axis=AX.X, op=ALU.min)
            V.op("sV", "tensor_tensor", out=hib[:, :], in0=hib[:, :], in1=t8a[:, :], op=ALU.min)
        V.fence("sV")
        T.wait("sV")
        T.op("sP", "matmul", pD[:, 8:16], ones1[0:1, :], lob[0:1, :], start=True, stop=True)
        T.fence("sP")
        V.wait("sP")
        V.op("sV", "tensor_copy", out=mid128[:, :], in_=pD[:, 8:16])
        V.fence("sV")
        GP.wait("sV")
        par_dma("sG", (gown[:, :], gown[:, :]), None,
                src01=(texp[:, 0:128], texp[:, 128:256]))
        GP.dma("sG", stats[:, 0:4], gown[:, 124:128])
        V.wait("sG")
        govw = rr(gown.ap(), "p (c e) -> p c e", c=16)
        V.op(None, "tensor_tensor", out=rr(selo.ap(), "p (c e) -> p c e", c=16), in0=govw,
             in1=mid128.ap().unsqueeze(1).broadcast_to([128, 16, 8]), op=ALU.is_gt)
        V.op(None, "tensor_single_scalar", out=hrdo[:, :], in_=gown[:, :], scalar=0.5, op=ALU.is_gt)
        V.op(None, "memset", accm[:, :], 0.0)
        sel3 = rr(selo.ap(), "p (c e) -> p c e", c=16)
        win3 = rr(wino.ap(), "p (c e) -> p c e", c=16)
        hrd3 = rr(hrdo.ap(), "p (c e) -> p c e", c=16)
        for e8 in range(7, -1, -1):
            V.op(None, "tensor_scalar", out=tmp16[:, :], in0=accm[:, :], scalar1=-1.0, scalar2=1.0,
                 op0=ALU.mult, op1=ALU.add)
            V.op(None, "tensor_tensor", out=win3[:, :, e8], in0=sel3[:, :, e8], in1=tmp16[:, :], op=ALU.mult)
            V.op(None, "tensor_tensor", out=accm[:, :], in0=accm[:, :], in1=sel3[:, :, e8], op=ALU.max)
        V.op(None, "memset", hwcol[:, :], 0.0)
        for e8 in range(8):
            V.op(None, "tensor_tensor", out=tmp16[:, :], in0=win3[:, :, e8], in1=hrd3[:, :, e8], op=ALU.mult)
            V.op(None, "tensor_add", out=hwcol[:, :], in0=hwcol[:, :], in1=tmp16[:, :])
        V.op("sV", "tensor_reduce", out=cnts[:, :], in_=rr(wino.ap(), "p (c e) -> p e c", c=16),
             axis=AX.X, op=ALU.add)
        V.fence("sV")
        T.wait("sV")
        T.op("sP", "matmul", pD[:, 8:16], tris[:, :], cnts[:, :], start=True, stop=True)
        T.fence("sP")
        V.wait("sP")
        V.op(None, "tensor_copy", out=pref[:, :], in_=pD[:, 8:16])
        V.op(None, "memset", slots[:, :], 0.0)
        for e8 in range(8):
            V.op(None, "tensor_tensor_scan", out=cum[:, :], data0=win3[:, :, e8], data1=win3[:, :, e8],
                 initial=0.0, op0=ALU.add, op1=ALU.bypass)
            V.op(None, "tensor_sub", out=cum[:, :], in0=cum[:, :], in1=win3[:, :, e8])
            V.op(None, "tensor_scalar", out=cum[:, :], in0=cum[:, :], scalar1=pref[:, e8:e8 + 1],
                 scalar2=float(e8 * 512), op0=ALU.add, op1=ALU.add)
            V.op(None, "tensor_tensor", out=cum[:, :], in0=cum[:, :], in1=win3[:, :, e8], op=ALU.mult)
            V.op(None, "tensor_add", out=slots[:, :], in0=slots[:, :], in1=cum[:, :])
        V.op(None, "tensor_scalar", out=tmp16[:, :], in0=accm[:, :], scalar1=-OOBF, scalar2=OOBF,
             op0=ALU.mult, op1=ALU.add)
        V.op(None, "tensor_add", out=slots[:, :], in0=slots[:, :], in1=tmp16[:, :])
        V.op("sV", "tensor_copy", out=sloti[:, :], in_=slots[:, :])
        V.fence("sV")
        if not gC:
            gB["_es"].close()
            esC = contextlib.ExitStack()
            gC["gTt0"] = esC.enter_context(nc.sbuf_tensor("gTt0", [128, 8 * 512], BF16))
            gC["gTt1"] = esC.enter_context(nc.sbuf_tensor("gTt1", [128, 8 * 512], BF16))
            gC["wep0"] = esC.enter_context(nc.sbuf_tensor("wep0", [128, D], BF16))
            gC["wep1"] = esC.enter_context(nc.sbuf_tensor("wep1", [128, D], BF16))
            gC["orow"] = esC.enter_context(nc.sbuf_tensor("orow", [128, D], F32))
            gC["oid"] = esC.enter_context(nc.sbuf_tensor("oidt", [128, 8], BF16))
            gC["ftile0"] = esC.enter_context(nc.sbuf_tensor("ftile0", [128, D], F32))
            gC["ftile1"] = esC.enter_context(nc.sbuf_tensor("ftile1", [128, D], F32))
        gTt = [gC["gTt0"], gC["gTt1"]]
        wep = [gC["wep0"], gC["wep1"]]
        orow, oid = gC["orow"], gC["oid"]
        ftile = [gC["ftile0"], gC["ftile1"]]
        rs_ring = []
        for c16 in range(NTC):
            fb = ftile[c16 % 2]
            if c16 >= 2:
                SY.wait("sV", rs_ring[c16 - 2])
            SY.dma("sQ", fb.ap().bitcast(BF16)[:, 0:1024], XL2D[c16 * 128:(c16 + 1) * 128, :])
            SY.dma("sQ", tailb[:, 8:12], fb.ap().bitcast(BF16)[:, 1020:1024])
            V.wait("sQ")
            if c16 >= 1:
                V.wait("sG")
            V.op(None, "tensor_scalar", out=spay[:, 0:1024], in0=fb.ap().bitcast(BF16)[:, 0:1024],
                 scalar1=hwcol[:, c16:c16 + 1], scalar2=None, op0=ALU.mult)
            V.op("sV", "tensor_copy", out=spay[:, 1024:1026], in_=idbf[:, 2 * c16:2 * c16 + 2])
            V.fence("sV")
            rs_ring.append(cnt["sV"])
            GP.wait("sV")
            GP.ind("sG", GBUF[:, :], IndirectOffsetOnAxis(sloti[:, c16:c16 + 1], 0),
                   spay[:, :], None, bcreg)
        GP.dma("sG", tailb[0:1, 4:8], GBUF[0:1, 0:4])
        scat_g = cnt["sG"]
        if _UPTO == "gbuf":
            GP.wait("sG")
            for tc in range(NTC):
                bx = xt[tc % 2]
                SY.wait("sG")
                SY.dma("sQ", bx.ap().bitcast(BF16)[:, 0:1028], GBUF[tc * 128:(tc + 1) * 128, :])
                SY.wait("sQ")
                SY.dma("sQ", Y[tc * 128:(tc + 1) * 128, :], bx[:, :])
                SY.wait("sQ")
            GP.wait("sQ")
            return
        T.wait("sV")
        ffn_vring = []
        ffn_evac_ring = []
        for e8 in range(8):
            gt = gTt[e8 % 2]
            if e8 >= 2:
                SC.wait("sP", ffn_vring[e8 - 2])
            if e8 >= 1:
                T.wait("sV", ffn_evac_ring[e8 - 1])
            SC.wait("sG", scat_g)
            for dc in range(8):
                SC.dma("sR", gt[:, dc * 512:(dc + 1) * 512],
                       GBUF[e8 * 512:(e8 + 1) * 512, dc * 128:(dc + 1) * 128], transpose=True)
            SC.dma("sR", tailb[:, 0:4], gt[:, 8 * 512 - 4:8 * 512])
            T.wait("sR")
            ic_ring = []
            for ic in range(8):
                wb = wep[ic % 2]
                if ic >= 2:
                    GP.wait("sP", ic_ring[ic - 2])
                GP.dma("sG", wb[:, :], EXPERTS[e8, ic * 128:(ic + 1) * 128, :])
                GP.dma("sG", tailb[:, 4:8], wb[:, 1020:1024])
                T.wait("sG")
                for g4 in range(4):
                    P = (pA, pB, pC, pD)[g4]
                    st, sp = ic == 0, ic == 7
                    lhs = gt[:, ic * 512 + g4 * 128:ic * 512 + (g4 + 1) * 128]
                    T.op(None, "matmul", P[:, 0:512], lhs, wb[:, 0:512], start=st, stop=sp)
                    T.op("sP" if g4 == 3 else None, "matmul",
                         P[:, 512:1024], lhs, wb[:, 512:1024], start=st, stop=sp)
                ic_ring.append(cnt["sP"])
            T.fence("sP")
            ffn_vring.append(cnt["sP"])
            V.wait("sP")
            V.wait("sQ")
            for g4 in range(4):
                P = (pA, pB, pC, pD)[g4]
                V.op("sV", "tensor_copy", out=orow[:, :], in_=P[:, :])
                V.fence("sV")
                SY.wait("sV")
                SY.dma("sQ", OBUF[e8 * 512 + g4 * 128:e8 * 512 + (g4 + 1) * 128, :], orow[:, :])
                if g4 < 3:
                    V.wait("sQ")
            ffn_evac_ring.append(cnt["sV"])
        SY.dma("sQ", stats[0:4, 4:8], OBUF[4092:4096, 1020:1024])
        GP.wait("sQ")
        allwait("sync", "vector", "gpsimd")
        p8_ring = []
        for tc in range(NTC):
            fb = ftile[tc % 2]
            bx = xt[tc % 2]
            if tc >= 2:
                V.wait("sQ")
                SY.wait("sV", p8_ring[tc - 2])
            V.op("sV", "memset", fb[:, :], 0.0)
            V.fence("sV")
            GP.wait("sV")
            GP.ind("sG", fb[:, :], None, OBUF[:, :],
                   IndirectOffsetOnAxis(sloti[:, tc:tc + 1], 0), bcreg)
            GP.dma("sG", stats[:, 0:4], fb[:, 1020:1024])
            SY.dma("sQ", bx[:, :], X2D[tc * 128:(tc + 1) * 128, :])
            SY.dma("sQ", stats[:, 4:8], bx[:, 1020:1024])
            V.wait("sQ")
            V.wait("sG")
            V.op("sV", "tensor_add", out=evac[:, 0:1024], in0=fb[:, :], in1=bx[:, :])
            V.fence("sV")
            p8_ring.append(cnt["sV"])
            SY.wait("sV")
            SY.dma("sQ", Y[tc * 128:(tc + 1) * 128, :], evac[:, 0:1024])
        GP.wait("sQ")
        if _DUMPT is not None:
            for sm in ("sV", "sA", "sP", "sG", "sR", "sQ"):
                SY.wait(sm)
            if _DUMPT == "STATS":
                for j, tl in enumerate((mcol, rcol, scol)):
                    SY.dma("sQ", Y[0:128, j * 16:(j + 1) * 16], tl[:, :])
                SY.wait("sQ")
                GP.wait("sQ")
                state["final_sQ"] = cnt["sQ"]
                return
            srcs = {"UT": (UT, "bf"), "X2D": (X2D, "f32"), "XL2D": (XL2D, "bfn"),
                    "OBUF": (OBUF, "f32"), "OBUF2": (OBUF, "f32h"),
                    "GBUF": (GBUF, "gb"), "XNT": (xnT, "sb")}
            src, fmt = srcs[_DUMPT]
            for tc in range(NTC):
                bx = xt[tc % 2]
                if fmt == "f32":
                    SY.dma("sQ", bx[:, :], src[tc * 128:(tc + 1) * 128, 0:1024])
                elif fmt == "f32h":
                    SY.dma("sQ", bx[:, :], src[2048 + tc * 128:2048 + (tc + 1) * 128, 0:1024])
                elif fmt == "bf":
                    SY.dma("sQ", bx.ap().bitcast(BF16)[:, 0:2048], src[tc * 128:(tc + 1) * 128, 0:2048])
                elif fmt == "bfn":
                    SY.dma("sQ", bx.ap().bitcast(BF16)[:, 0:1024], src[tc * 128:(tc + 1) * 128, 0:1024])
                    SY.dma("sQ", bx.ap().bitcast(BF16)[:, 1024:2048], src[tc * 128:(tc + 1) * 128, 0:1024])
                elif fmt == "gb":
                    SY.dma("sQ", bx.ap().bitcast(BF16)[:, 0:1028], src[tc * 128:(tc + 1) * 128, 0:1028])
                elif fmt == "sb":
                    if tc < 8:
                        SY.dma("sQ", bx[:, :], src.ap().bitcast(F32)[:, tc * 1024:(tc + 1) * 1024])
                    else:
                        SY.dma("sQ", bx[:, :], X2D[tc * 128:(tc + 1) * 128, 0:1024])
                SY.wait("sQ")
                SY.dma("sQ", Y[tc * 128:(tc + 1) * 128, :], bx[:, :])
                SY.wait("sQ")
            GP.wait("sQ")
        state["final_sQ"] = cnt["sQ"]

    with nc.Block() as block:
        regs = {"sync": block.sync, "gpsimd": block.gpsimd, "vector": block.vector,
                "tensor": block.tensor, "scalar": block.scalar}
        for en in ENGS:
            def body(h, en=en):
                cnt = {}
                chains = {"vector": "sV", "scalar": "sA"}
                px = {e: Pxy(cnt, sems, h if e == en else None, chains.get(e)) for e in ENGS}
                try:
                    program(px)
                except Exception:
                    import traceback
                    print("FAILED in engine pass:", en)
                    traceback.print_exc()
                    raise
            regs[en](body)
    return nc


_NC_CACHE = {}


def _host_inputs(x, g, uv_w, gamma, beta, w_rel, o_w, res_scale, gate_w, experts):
    f32 = np.float32
    uvwT = np.ascontiguousarray(uv_w.T).astype(f32, copy=False)
    owT = np.ascontiguousarray(o_w.T).astype(f32, copy=False)
    gb4 = np.ascontiguousarray(np.stack([gamma[0], beta[0], gamma[1], beta[1]], axis=1)).astype(f32)
    gsc = np.array([[float(g[0]), 0.0]], f32)
    ress = res_scale.reshape(1, D).astype(f32, copy=False)
    inv_freq = (10000.0 ** (np.arange(64, dtype=np.float64) / 64)).astype(f32)
    pos = np.arange(L, dtype=f32)[:, None]
    sinu = (pos * inv_freq[None, :]).astype(f32)
    SINf = np.sin(sinu).astype(f32)
    COSf = np.cos(sinu).astype(f32)
    def w2_for(h):
        m = np.arange(4608) + (1 - h) * 2048
        return w_rel[np.clip(m, 0, 2046)].astype(f32).reshape(1, 4608)
    ids_lo = (np.arange(NTC)[None, :] * 128 + np.arange(128)[:, None]).astype(np.int32)
    tri = (np.arange(128)[:, None] < np.arange(128)[None, :]).astype(f32)
    ident = np.eye(128, dtype=f32)
    tag = np.random.randint(1 << 20, 1 << 30, size=3).astype(np.int32)
    tags = np.array([[tag[0], 0, tag[1], 0, tag[2], 0]], np.int32)
    ins = []
    for c in range(8):
        b, h = c // 2, c % 2
        t0 = h * HALF
        ins.append({
            "x_own": np.ascontiguousarray(x[b, t0:t0 + HALF, :]).astype(f32, copy=False),
            "uvwT": uvwT,
            "owT": owT,
            "experts": experts.astype(f32, copy=False),
            "gateW": np.ascontiguousarray(gate_w).astype(f32, copy=False),
            "gb4": gb4,
            "gsc": gsc,
            "res_scale": ress,
            "w2": w2_for(h),
            "cosT": np.ascontiguousarray(COSf[t0:t0 + HALF, :].T),
            "sinT": np.ascontiguousarray(SINf[t0:t0 + HALF, :].T),
            "ids_lo": ids_lo,
            "tags": tags,
            "tri": tri,
            "ident": ident,
        })
    return ins


def kernel(L0_attn_x, L0_attn_g, L0_attn_uv_w, L0_attn_gamma, L0_attn_beta,
           L0_attn_w_rel, L0_attn_o_w, L0_attn_res_scale, L0_ffn_gate_w,
           L0_ffn_experts):
    x = np.asarray(L0_attn_x, np.float32)
    ins = _host_inputs(x, np.asarray(L0_attn_g, np.float32),
                       np.asarray(L0_attn_uv_w, np.float32),
                       np.asarray(L0_attn_gamma, np.float32),
                       np.asarray(L0_attn_beta, np.float32),
                       np.asarray(L0_attn_w_rel, np.float32),
                       np.asarray(L0_attn_o_w, np.float32),
                       np.asarray(L0_attn_res_scale, np.float32),
                       np.asarray(L0_ffn_gate_w, np.float32)[0],
                       np.asarray(L0_ffn_experts, np.float32))
    if "nc" not in _NC_CACHE:
        _NC_CACHE["nc"] = build()
    res = run_bass_kernel_spmd(_NC_CACHE["nc"], ins, core_ids=list(range(8)))
    out = np.empty((B, L, D), np.float32)
    for c in range(8):
        b, h = c // 2, c % 2
        out[b, h * HALF:(h + 1) * HALF, :] = res.results[c]["y_own"]
    return out


if __name__ == "__main__":
    nc = build()
    print("build ok:", nc.n_instructions(), "instructions")


# version 13 (final)
# speedup vs baseline: 1.0222x; 1.0222x over previous
import contextlib
import numpy as np
import ml_dtypes
import concourse.bass as bass
import concourse.mybir as mybir
from concourse.ap import AP as APc
from concourse.bass_core_rust import IndirectOffsetOnAxis
from concourse.bass_utils import run_bass_kernel_spmd

F32 = mybir.dt.float32
BF16 = mybir.dt.bfloat16
I32 = mybir.dt.int32
AF = mybir.ActivationFunctionType
ALU = mybir.AluOpType
AX = mybir.AxisListType
ET = mybir.EngineType

B, L, D, S, E = 4, 4096, 1024, 128, 2048
HALF, NTC, QB = 2048, 16, 1024
EPS1 = 1e-5
INV_SQRT_S = float(1.0 / np.sqrt(128.0))
OOBF = float(1 << 28)
PAD_BF = float(np.array([0x7F7F], np.uint16).view(ml_dtypes.bfloat16)[0])
ENGS = ("sync", "gpsimd", "vector", "tensor", "scalar")
_UPTO = None
_DUMPT = None
BAND_KC = {0: list(range(8)), 1: list(range(16))}


class Pxy:
    def __init__(self, cnt, sems, handle, chain_sem=None):
        self.c = cnt
        self.sems = sems
        self.h = handle
        self.chain_sem = chain_sem

    def dma(self, sem, out, in_, **kw):
        self.c[sem] = self.c.get(sem, 0) + 16
        if self.h is not None:
            self.h.dma_start(out=out, in_=in_, **kw).then_inc(self.sems[sem], 16)

    def ind(self, sem, out, out_off, in_, in_off, bc):
        self.c[sem] = self.c.get(sem, 0) + 16
        if self.h is not None:
            self.h.indirect_dma_start(
                out=out, out_offset=out_off, in_=in_, in_offset=in_off,
                bounds_check=bc, oob_is_err=False,
            ).then_inc(self.sems[sem], 16)

    def op(self, sem, meth, *a, **kw):
        chain = self.chain_sem
        if chain is not None:
            sem = chain
            prev = self.c.get(sem, 0)
        if sem is not None:
            self.c[sem] = self.c.get(sem, 0) + 1
        if self.h is not None:
            if chain is not None and prev > 0:
                self.h.wait_ge(self.sems[sem], prev)
            ins = getattr(self.h, meth)(*a, **kw)
            if sem is not None:
                ins.then_inc(self.sems[sem], 1)

    def wait(self, sem, val=None):
        if self.h is not None:
            v = self.c.get(sem, 0) if val is None else val
            if v > 0:
                self.h.wait_ge(self.sems[sem], v)

    def fence(self, sem):
        if self.chain_sem is not None:
            sem = self.chain_sem
        self.c[sem] = self.c.get(sem, 0) + 1
        if self.h is not None:
            self.h.drain().then_inc(self.sems[sem], 1)


def rr(ap, spec, **kw):
    return ap.rearrange(spec, **kw)


def build():
    nc = bass.Bass("TRN2", target_bir_lowering=False,
                   detect_race_conditions=False, disable_frame_to_traceback=False)

    X = nc.dram_tensor("x_own", [HALF, D], F32, kind="ExternalInput").ap()
    UVWT = nc.dram_tensor("uvwT", [D, 4224], F32, kind="ExternalInput").ap()
    OWT = nc.dram_tensor("owT", [E, D], F32, kind="ExternalInput").ap()
    EXPERTS = nc.dram_tensor("experts", [8, D, D], F32, kind="ExternalInput").ap()
    GATEW = nc.dram_tensor("gateW", [D, 8], F32, kind="ExternalInput").ap()
    GB4 = nc.dram_tensor("gb4", [S, 4], F32, kind="ExternalInput").ap()
    GSC = nc.dram_tensor("gsc", [1, 2], F32, kind="ExternalInput").ap()
    RESS = nc.dram_tensor("res_scale", [1, D], F32, kind="ExternalInput").ap()
    W2 = nc.dram_tensor("w2", [1, 4608], F32, kind="ExternalInput").ap()
    COST = nc.dram_tensor("cosT", [64, HALF], F32, kind="ExternalInput").ap()
    SINT = nc.dram_tensor("sinT", [64, HALF], F32, kind="ExternalInput").ap()
    IDS = nc.dram_tensor("ids_lo", [128, NTC], I32, kind="ExternalInput").ap()
    TAGS = nc.dram_tensor("tags", [1, 6], I32, kind="ExternalInput").ap()
    TRI = nc.dram_tensor("tri", [128, 128], F32, kind="ExternalInput").ap()
    IDEN = nc.dram_tensor("ident", [128, 128], F32, kind="ExternalInput").ap()
    Y = nc.dram_tensor("y_own", [HALF, D], F32, kind="ExternalOutput").ap()

    SHV = nc.dram_tensor("shv", [L, E], BF16, kind="Internal", addr_space="Shared").ap()
    SHK = nc.dram_tensor("shk", [2, 128, HALF], BF16, kind="Internal", addr_space="Shared").ap()
    SHL = nc.dram_tensor("shl", [L, 8], F32, kind="Internal", addr_space="Shared").ap()
    FLG = nc.dram_tensor("flg", [2, 8], I32, kind="Internal", addr_space="Shared").ap()

    UT = nc.dram_tensor("uT", [E, HALF], BF16, kind="Internal").ap()
    X2D = nc.dram_tensor("x2d", [HALF, D], F32, kind="Internal").ap()
    XL2D = nc.dram_tensor("xl2d", [HALF, D], BF16, kind="Internal").ap()
    GBUF = nc.dram_tensor("gbuf", [4096, 1028], BF16, kind="Internal").ap()
    OBUF = nc.dram_tensor("obuf", [4096, D], F32, kind="Internal").ap()

    es = contextlib.ExitStack()
    sb = lambda n, shp, dt: es.enter_context(nc.sbuf_tensor(n, shp, dt))
    psal = lambda n, shp: es.enter_context(nc.psum_tensor(n, shp, F32))

    qT = sb("qT", [128, HALF], BF16)
    kTt = sb("kTt", [128, L], BF16)
    xt = [sb(f"xt{i}", [128, D], F32) for i in range(2)]
    evac = sb("evacs", [128, D], F32)
    scr = sb("scr", [128, QB], F32)
    scr2 = sb("scr2", [128, QB], F32)
    stats = sb("stats", [128, 8], F32)
    mcol = sb("mcol", [128, NTC], F32)
    rcol = sb("rcol", [128, NTC], F32)
    scol = sb("scol", [128, NTC], F32)
    m2col = sb("m2col", [128, NTC], F32)
    r2col = sb("r2col", [128, NTC], F32)
    gb4s = sb("gb4s", [128, 4], F32)
    gscs = sb("gscs", [1, 2], F32)
    ressr = sb("ressr", [1, D], F32)
    resb = sb("resb", [128, D], F32)
    idenb = sb("idenb", [128, 128], F32)
    tris = sb("tris", [128, 128], F32)
    ones1 = sb("ones1", [1, 128], F32)
    onesc = sb("onesc", [128, 1], F32)
    cbias = sb("cbias", [128, 1], F32)
    epsb = sb("epsb", [128, 1], F32)
    cm05 = sb("cm05", [128, 1], F32)
    cisq = sb("cisq", [128, 1], F32)
    gb128 = sb("gb128", [128, 1], F32)
    w2one = sb("w2one", [1, 1], F32)
    tags = sb("tagss", [1, 6], I32)
    tailb = sb("tailb", [128, 16], BF16)
    fbuf = sb("fbuf", [1, 8], I32)
    idss = sb("idss", [128, NTC], I32)
    idbf = sb("idbf", [128, 2 * NTC], BF16)
    zzf = sb("zzf", [128, D], F32)
    padp = sb("padp", [128, 4], BF16)
    bandt = sb("bandt", [128, QB], F32)
    gatewb = sb("gatewb", [128, 64], BF16)
    tlog = sb("tlog", [128, 256], F32)
    texp = sb("texp", [128, 256], F32)
    red32 = sb("red32", [128, 32], F32)
    red8 = sb("red8", [128, 8], F32)
    row8 = sb("row8", [1, 8], F32)
    lob = sb("lob", [1, 8], F32)
    hib = sb("hib", [1, 8], F32)
    midb = sb("midb", [1, 8], F32)
    t8a = sb("t8a", [1, 8], F32)
    kfrac = sb("kfrac", [1, 120], F32)
    candr = sb("candr", [1, 120], F32)
    rowK = sb("rowK", [1, 120], F32)
    gK = sb("gK", [1, 120], F32)
    cK2 = sb("cK2", [1, 120], F32)
    mid128 = sb("mid128", [128, 8], F32)
    gown = sb("gown", [128, 128], F32)
    selo = sb("selo", [128, 128], F32)
    hrdo = sb("hrdo", [128, 128], F32)
    wino = sb("wino", [128, 128], F32)
    accm = sb("accm", [128, 16], F32)
    cum = sb("cum", [128, 16], F32)
    tmp16 = sb("tmp16", [128, 16], F32)
    slots = sb("slots", [128, NTC], F32)
    sloti = sb("sloti", [128, NTC], I32)
    hwcol = sb("hwcol", [128, NTC], F32)
    cnts = sb("cnts", [128, 8], F32)
    pref = sb("pref", [128, 8], F32)
    spay = sb("spay", [128, 1028], BF16)
    xl2r = [sb(f"xl2r{i}", [128, D], BF16) for i in range(3)]
    xlt = sb("xlt", [128, D], BF16)
    idb16 = sb("idb16", [128, 128], BF16)
    vstage = [sb(f"vstage{i}", [128, 256], BF16) for i in range(4)]
    ustage = [sb(f"ustage{i}", [128, QB], BF16) for i in range(2)]
    wpan = [sb(f"wpan{i}", [128, 8 * 128], BF16) for i in range(2)]

    esA2 = contextlib.ExitStack()
    sbA2 = lambda n, shp, dt: esA2.enter_context(nc.sbuf_tensor(n, shp, dt))
    basT = sbA2("basT", [128, HALF], F32)
    cosS = sbA2("cosS", [64, HALF], F32)
    sinS = sbA2("sinS", [64, HALF], F32)
    xnf = sbA2("xnf", [128, D], F32)
    esA1 = contextlib.ExitStack()
    sbA1 = lambda n, shp, dt: esA1.enter_context(nc.sbuf_tensor(n, shp, dt))
    xnT = sbA1("xnT", [128, 8 * HALF], BF16)
    wvres = sbA1("wvres", [128, 8 * E], BF16)
    esA3 = contextlib.ExitStack()
    rlo = esA3.enter_context(nc.sbuf_tensor("rlo", [64, HALF], F32))
    rhi = esA3.enter_context(nc.sbuf_tensor("rhi", [64, HALF], F32))
    rql = esA3.enter_context(nc.sbuf_tensor("rql", [64, HALF], F32))
    rqh = esA3.enter_context(nc.sbuf_tensor("rqh", [64, HALF], F32))

    gB = {}
    gC = {}

    pA = psal("pA", [128, 1024])
    pB = psal("pB", [128, 1024])
    pC = psal("pC", [128, 1024])
    pD = psal("pD", [128, 1024])

    sems = {}
    for s in ("sQ", "sR", "sG", "sP", "sV", "sA", "sF"):
        sems[s] = es.enter_context(nc.semaphore(s))

    state = {}

    def program(px):
        SY, GP, V, T, SC = (px[e] for e in ENGS)
        cnt = SY.c

        if GP.h is not None:
            pidv = nc.partition_id(engines=[ET.Pool])
            hpar = pidv % 2
            kreg = nc.alloc_register(ET.Pool, "pollk")
            GP.h.reg_mov(kreg, 0)
            hreg = nc.alloc_register(ET.Pool, "hreg")
            GP.h.reg_alu(hreg, pidv, 2, ALU.mod)
            bcreg = nc.alloc_register(ET.Pool, "bcreg")
            GP.h.reg_mov(bcreg, 4095)
        else:
            hpar = None
            bcreg = 4095

        def dump_and_end(tiles):
            SY.wait("sV"); SY.wait("sA"); SY.wait("sP"); SY.wait("sG"); SY.wait("sR")
            for i, ap in enumerate(tiles[:NTC]):
                w = ap.shape[-1]
                SY.dma("sQ", Y[i * 128:(i + 1) * 128, 0:w], ap)
            GP.wait("sQ")
            SY.wait("sQ")

        def par_dma(sem, dst01, src_ap, src01=None):
            cnt[sem] = cnt.get(sem, 0) + 16
            if GP.h is None:
                return
            with GP.h.If_eq(hreg, 0):
                GP.h.dma_start(out=dst01[0], in_=src_ap if src01 is None else src01[0]).then_inc(sems[sem], 16)
            with GP.h.Else():
                GP.h.dma_start(out=dst01[1], in_=src_ap if src01 is None else src01[1]).then_inc(sems[sem], 16)

        def allwait(*targets):
            for eng in targets:
                for sm in ("sQ", "sR", "sG", "sP", "sV", "sA"):
                    px[eng].wait(sm)

        def sym_dst(tensor, off_expr, ap):
            return APc(tensor, off_expr, ap)

        SY.dma("sQ", gb4s[:, :], GB4[:, :])
        SY.dma("sQ", gscs[:, :], GSC[:, :])
        SY.dma("sQ", ressr[:, :], RESS[:, :])
        SY.dma("sQ", w2one[:, :], W2[0:1, 4095:4096])
        SY.dma("sQ", cosS[:, :], COST[:, :])
        SY.dma("sQ", sinS[:, :], SINT[:, :])
        SY.dma("sQ", idenb[:, :], IDEN[:, :])
        SY.dma("sQ", tris[:, :], TRI[:, :])
        SY.dma("sQ", tags[:, :], TAGS[:, :])
        SY.dma("sQ", idss[:, :], IDS[:, :])
        SY.dma("sQ", stats.ap().bitcast(I32)[:, 4:5], idss[:, NTC - 1:NTC])
        GP.dma("sG", rr(gatewb.ap(), "p (a e) -> p a e", a=8), rr(GATEW, "(a p) e -> p a e", p=128))
        V.op("sV", "memset", epsb[:, :], EPS1)
        V.op("sV", "memset", cm05[:, :], -0.5)
        V.op("sV", "memset", cisq[:, :], INV_SQRT_S)
        V.op("sV", "memset", ones1[:, :], 1.0)
        V.op("sV", "memset", onesc[:, :], 1.0)
        V.op("sV", "memset", zzf[:, :], 0.0)
        V.op("sV", "memset", padp[:, :], PAD_BF)
        V.op("sV", "memset", spay[:, 1026:1028], 0.0)
        for k in range(15):
            V.op(None, "memset", kfrac[:, k * 8:(k + 1) * 8], (k + 1) / 16.0)
        V.wait("sQ")
        V.op(None, "tensor_copy", out=idb16[:, :], in_=idenb[:, :])
        V.op("sV", "tensor_copy", out=idbf[:, :], in_=idss.ap().bitcast(BF16)[:, :])
        T.wait("sQ")
        T.wait("sV")
        T.op(None, "matmul", pC[:, 0:512], ones1[0:1, :], ressr[0:1, 0:512], start=True, stop=True)
        T.op(None, "matmul", pC[:, 512:1024], ones1[0:1, :], ressr[0:1, 512:1024], start=True, stop=True)
        T.op(None, "matmul", pD[:, 0:1], ones1[0:1, :], w2one[0:1, 0:1], start=True, stop=True)
        T.op("sP", "matmul", pD[:, 1:2], ones1[0:1, :], gscs[0:1, 0:1], start=True, stop=True)
        T.fence("sP")
        V.wait("sP")
        V.op(None, "tensor_copy", out=resb[:, :], in_=pC[:, :])
        V.op(None, "tensor_copy", out=gb128[:, :], in_=pD[:, 1:2])
        V.op("sV", "tensor_scalar_mul", out=cbias[:, :], in0=pD[:, 0:1], scalar1=INV_SQRT_S)
        V.fence("sV")
        v_prolog = cnt["sV"]

        sum_ring = []
        for tc in range(NTC):
            buf = xt[tc % 2]
            if tc >= 2:
                SY.wait("sV", sum_ring[tc - 2])
            SY.dma("sQ", buf[:, :], X[tc * 128:(tc + 1) * 128, :])
            SY.dma("sQ", stats[:, 4:8], buf[:, 1020:1024])
            V.wait("sQ")
            SC.wait("sQ")
            if tc >= 1:
                SC.wait("sV")
            SC.op("sA", "activation", out=scr[:, 0:1024], in_=buf[:, :], func=AF.Square)
            SC.fence("sA")
            V.op(None, "reduce_sum", out=mcol[:, tc:tc + 1], in_=buf[:, :], axis=AX.X)
            V.wait("sA")
            V.op("sV", "reduce_sum", out=rcol[:, tc:tc + 1], in_=scr[:, 0:1024], axis=AX.X)
            sum_ring.append(cnt["sV"])
        V.op(None, "tensor_scalar_mul", out=mcol[:, :], in0=mcol[:, :], scalar1=1.0 / D)
        V.op(None, "tensor_scalar_mul", out=rcol[:, :], in0=rcol[:, :], scalar1=1.0 / D)
        V.op(None, "tensor_mul", out=cum[:, :], in0=mcol[:, :], in1=mcol[:, :])
        V.op("sV", "tensor_sub", out=rcol[:, :], in0=rcol[:, :], in1=cum[:, :])
        V.fence("sV")
        SC.wait("sV")
        SC.op(None, "activation", out=tmp16[:, :], in_=rcol[:, :], func=AF.Ln, bias=epsb[:, :])
        SC.op("sA", "activation", out=cum[:, :], in_=rcol[:, :], func=AF.Sqrt)
        SC.wait("sA")
        SC.op("sA", "activation", out=rcol[:, :], in_=tmp16[:, :], func=AF.Exp, scale=cm05[:, :])
        SC.fence("sA")
        V.wait("sA")
        V.op(None, "tensor_mul", out=cum[:, :], in0=cum[:, :], in1=rcol[:, :])
        V.op(None, "tensor_scalar_max", out=cum[:, :], in0=cum[:, :], scalar1=EPS1)
        V.op("sV", "reciprocal", out=cum[:, :], in_=cum[:, :])
        V.fence("sV")
        SC.wait("sV")
        SC.op("sA", "mul", out=cum[:, :], in_=cum[:, :], mul=gb128[:, :])
        SC.fence("sA")
        V.wait("sA")
        V.op("sV", "tensor_mul", out=scol[:, :], in0=cum[:, :], in1=rcol[:, :])
        V.fence("sV")
        ln1_ring = []
        for tc in range(NTC):
            buf = xt[tc % 2]
            if tc >= 2:
                SY.wait("sV", ln1_ring[tc - 2])
            else:
                SY.wait("sV", sum_ring[NTC - 2 + tc])
            SY.dma("sQ", buf[:, :], X[tc * 128:(tc + 1) * 128, :])
            SY.dma("sQ", stats[:, 4:8], buf[:, 1020:1024])
            V.wait("sQ")
            V.op("sV", "tensor_scalar", out=xnf[:, :], in0=buf[:, :], scalar1=mcol[:, tc:tc + 1],
                 scalar2=scol[:, tc:tc + 1], op0=ALU.subtract, op1=ALU.mult)
            V.fence("sV")
            ln1_ring.append(cnt["sV"])
            T.wait("sV")
            for dc in range(8):
                T.op("sP" if dc == 7 else None, "transpose",
                     rr(pA.ap(), "p (a b) -> p a b", a=8)[:, dc, :],
                     xnf[:, dc * 128:(dc + 1) * 128], idenb[:, :])
            T.fence("sP")
            V.wait("sP")
            V.op("sV", "tensor_copy",
                 out=rr(xnT.ap(), "p (a b) -> p a b", a=8)[:, :, tc * 128:(tc + 1) * 128],
                 in_=rr(pA.ap(), "p (a b) -> p a b", a=8)[:, :, :])
            V.fence("sV")
            T.wait("sV")
        for dc in range(8):
            GP.dma("sG", wvres[:, dc * E:(dc + 1) * E], UVWT[dc * 128:(dc + 1) * 128, E:2 * E])
        g_wv = cnt["sG"]

        uv_pring, uv_aring, uv_qring = [], [], []
        for cc in range(17):
            col0 = cc * 128 if cc < 16 else 4096
            pa, pb = (pA, pB) if cc % 2 == 0 else (pC, pD)
            wbuf = wpan[cc % 2]
            if cc >= 2:
                GP.wait("sP", uv_pring[cc - 2])
            GP.dma("sG", rr(wbuf[:, :], "p (a c) -> p a c", a=8), rr(UVWT[:, col0:col0 + 128], "(a p) c -> p a c", p=128))
            GP.dma("sG", tailb[:, 4:8], wbuf[:, 8 * 128 - 4:8 * 128])
            T.wait("sG")
            if cc >= 2:
                T.wait("sA", uv_aring[cc - 2])
            for dc in range(8):
                st, sp = dc == 0, dc == 7
                wsl = wbuf[:, dc * 128:(dc + 1) * 128]
                for half in range(2):
                    P = pa if half == 0 else pb
                    for nn2 in range(2):
                        t0 = half * 1024 + nn2 * 512
                        T.op("sP" if (half == 1 and nn2 == 1) else None, "matmul",
                             P[:, nn2 * 512:nn2 * 512 + 512], wsl,
                             xnT[:, dc * HALF + t0:dc * HALF + t0 + 512], start=st, stop=sp)
            uv_pring.append(cnt["sP"])
            if cc == 16:
                T.fence("sP")
                SC.wait("sP")
            else:
                SC.wait("sP", uv_pring[-1] + 1)
            if cc < 16:
                if cc >= 1:
                    SC.wait("sQ", uv_qring[cc - 1])
                SC.op(None, "activation", out=evac.ap().bitcast(BF16)[:, 0:1024], in_=pa[:, :], func=AF.Silu)
                SC.op("sA", "activation", out=evac.ap().bitcast(BF16)[:, 1024:2048], in_=pb[:, :], func=AF.Silu)
                SC.fence("sA")
                uv_aring.append(cnt["sA"])
                SY.wait("sA")
                SY.dma("sQ", UT[cc * 128:(cc + 1) * 128, :], evac.ap().bitcast(BF16)[:, 0:2048])
                uv_qring.append(cnt["sQ"])
            else:
                SC.op(None, "activation", out=basT[:, 0:1024], in_=pa[:, :], func=AF.Silu)
                SC.op("sA", "activation", out=basT[:, 1024:2048], in_=pb[:, :], func=AF.Silu)
                SC.fence("sA")
                uv_aring.append(cnt["sA"])
        a_base = cnt["sA"]

        if _UPTO == "p2a":
            SY.wait("sA", a_base)
            dump_and_end([basT[:, 0:1024], basT[:, 1024:2048],
                          xnT.ap().bitcast(F32)[:, 0:1024], xnT.ap().bitcast(F32)[:, 1024:2048]])
            return
        V.wait("sA", a_base)
        for br in range(2):
            g0, b0 = (0, 1) if br == 0 else (2, 3)
            V.wait("sQ")
            V.op(None, "tensor_scalar", out=scr[:, :], in0=basT[:, 0:QB], scalar1=gb4s[:, g0:g0 + 1],
                 scalar2=gb4s[:, b0:b0 + 1], op0=ALU.mult, op1=ALU.add)
            V.op("sV", "tensor_scalar", out=scr2[:, :], in0=basT[:, QB:2 * QB], scalar1=gb4s[:, g0:g0 + 1],
                 scalar2=gb4s[:, b0:b0 + 1], op0=ALU.mult, op1=ALU.add)
            V.fence("sV")
            SY.wait("sV")
            SY.dma("sQ", rlo[:, 0:QB], scr[0:64, :])
            SY.dma("sQ", rlo[:, QB:2 * QB], scr2[0:64, :])
            SY.dma("sQ", rhi[:, 0:QB], scr[64:128, :])
            SY.dma("sQ", rhi[:, QB:2 * QB], scr2[64:128, :])
            SY.dma("sQ", stats[0:64, 4:8], rhi[:, 2 * QB - 4:2 * QB])
            V.wait("sQ")
            V.op(None, "tensor_mul", out=rql[:, :], in0=rlo[:, :], in1=cosS[:, :])
            V.op(None, "tensor_mul", out=rqh[:, :], in0=rhi[:, :], in1=sinS[:, :])
            V.op(None, "tensor_sub", out=rql[:, :], in0=rql[:, :], in1=rqh[:, :])
            V.op(None, "tensor_mul", out=rqh[:, :], in0=rhi[:, :], in1=cosS[:, :])
            V.op(None, "tensor_mul", out=rlo[:, :], in0=rlo[:, :], in1=sinS[:, :])
            V.op(None, "tensor_add", out=rqh[:, :], in0=rqh[:, :], in1=rlo[:, :])
            V.op(None, "tensor_copy", out=scr.ap().bitcast(BF16)[0:64, 0:HALF], in_=rql[:, :])
            V.op("sV", "tensor_copy", out=scr2.ap().bitcast(BF16)[0:64, 0:HALF], in_=rqh[:, :])
            V.fence("sV")
            SY.wait("sV")
            if br == 0:
                SY.dma("sQ", qT[0:64, :], scr.ap().bitcast(BF16)[0:64, 0:HALF])
                SY.dma("sQ", qT[64:128, :], scr2.ap().bitcast(BF16)[0:64, 0:HALF])
                SY.dma("sQ", tailb[0:64, 8:12], qT[64:128, HALF - 4:HALF])
            else:
                SY.dma("sQ", kTt[0:64, 0:HALF], scr.ap().bitcast(BF16)[0:64, 0:HALF])
                SY.dma("sQ", kTt[64:128, 0:HALF], scr2.ap().bitcast(BF16)[0:64, 0:HALF])
                SY.dma("sQ", tailb[0:64, 8:12], kTt[64:128, HALF - 4:HALF])
        GP.wait("sQ")
        par_dma("sG", (APc(SHK.tensor, 0, [[HALF, 128], [1, HALF]]),
                       APc(SHK.tensor, 128 * HALF, [[HALF, 128], [1, HALF]])),
                kTt[:, 0:HALF])
        GP.wait("sG")
        par_dma("sG", (APc(FLG.tensor, 0, [[8, 1], [1, 2]]),
                       APc(FLG.tensor, 8, [[8, 1], [1, 2]])),
                tags[:, 0:2])
        v_pring, v_aring, v_gring = [], [], []
        T.wait("sG", g_wv)
        T.wait("sA", a_base)
        for tc in range(NTC):
            pa, pb = (pA, pB) if tc % 2 == 0 else (pC, pD)
            if tc >= 2:
                T.wait("sA", v_aring[tc - 2])
            for dc in range(8):
                st, sp = dc == 0, dc == 7
                lhs = xnT[:, dc * HALF + tc * 128:dc * HALF + (tc + 1) * 128]
                for half in range(2):
                    P = pa if half == 0 else pb
                    for nn2 in range(2):
                        e0 = half * 1024 + nn2 * 512
                        T.op("sP" if (half == 1 and nn2 == 1) else None, "matmul",
                             P[:, nn2 * 512:nn2 * 512 + 512], lhs,
                             wvres[:, dc * E + e0:dc * E + e0 + 512], start=st, stop=sp)
            v_pring.append(cnt["sP"])
            if tc == NTC - 1:
                T.fence("sP")
                SC.wait("sP")
            else:
                SC.wait("sP", v_pring[-1] + 1)
            if tc >= 1:
                SC.wait("sG", v_gring[tc - 1])
            SC.op(None, "activation", out=evac.ap().bitcast(BF16)[:, 0:1024], in_=pa[:, :], func=AF.Silu)
            SC.op("sA", "activation", out=evac.ap().bitcast(BF16)[:, 1024:2048], in_=pb[:, :], func=AF.Silu)
            SC.fence("sA")
            v_aring.append(cnt["sA"])
            GP.wait("sA")
            par_dma("sG", (SHV[tc * 128:(tc + 1) * 128, :],
                           SHV[HALF + tc * 128:HALF + (tc + 1) * 128, :]),
                    evac.ap().bitcast(BF16)[:, 0:2048])
            v_gring.append(cnt["sG"])

        GP.wait("sG")
        par_dma("sG", (APc(FLG.tensor, 2, [[8, 1], [1, 2]]),
                       APc(FLG.tensor, 10, [[8, 1], [1, 2]])),
                tags[:, 2:4])

        if not gB:
            esA3.close()
            esA1.close()
            esA2.close()
            esB = contextlib.ExitStack()
            gB["kerT"] = esB.enter_context(nc.sbuf_tensor("kerT", [128, 32 * QB], BF16))
            gB["uaq"] = esB.enter_context(nc.sbuf_tensor("uaq", [128, 16 * QB], BF16))
            gB["owts"] = esB.enter_context(nc.sbuf_tensor("owts", [128, 16 * 1024], BF16))
            gB["_es"] = esB
        kerT, uaq, owts = gB["kerT"], gB["uaq"], gB["owts"]

        for ecc in range(16):
            GP.dma("sG", owts[:, ecc * 1024:(ecc + 1) * 1024], OWT[ecc * 128:(ecc + 1) * 128, :])
        g_ow = cnt["sG"]
        if GP.h is not None:
            psrcK = sym_dst(FLG.tensor, (1 - hpar) * 8, [[8, 1], [1, 2]])
            expK = GP.h.value_load(tags.ap()[0:1, 0:1])

            def pollK():
                GP.h.reg_add(kreg, kreg, 16)
                GP.h.dma_start(out=fbuf[:, 0:2], in_=psrcK).then_inc(sems["sF"], 16)
                GP.h.wait_ge(sems["sF"], kreg)
                val = GP.h.value_load(fbuf.ap()[0:1, 0:1])
                return (val - expK) != 0

            with GP.h.While(pollK):
                GP.h.nop()
        GP.dma("sG", kTt[:, 0:HALF], SHK[0, :, :])
        GP.dma("sG", kTt[:, HALF:L], SHK[1, :, :])
        GP.dma("sG", tailb[:, 4:8], kTt[:, L - 4:L])
        g_kt = cnt["sG"]
        if GP.h is not None:
            psrcV = sym_dst(FLG.tensor, (1 - hpar) * 8 + 2, [[8, 1], [1, 2]])
            expV = GP.h.value_load(tags.ap()[0:1, 2:3])

            def pollV():
                GP.h.reg_add(kreg, kreg, 16)
                GP.h.dma_start(out=fbuf[:, 2:4], in_=psrcV).then_inc(sems["sF"], 16)
                GP.h.wait_ge(sems["sF"], kreg)
                val = GP.h.value_load(fbuf.ap()[0:1, 2:3])
                return (val - expV) != 0

            with GP.h.While(pollV):
                GP.h.nop()
        GP.dma("sG", fbuf[:, 4:6], APc(FLG.tensor, 4, [[8, 1], [1, 2]]))
        g_vok = cnt["sG"]
        GP.wait("sV", v_prolog)
        for i in range(32):
            GP.dma("sG", GBUF[i * 128:(i + 1) * 128, 0:1024], zzf.ap().bitcast(BF16)[:, 0:1024])
            GP.dma("sG", GBUF[i * 128:(i + 1) * 128, 1024:1028], padp[:, :])

        shl_ring = []
        xq_ring = []
        xls_v, tp_p, xlt_a, gate_p, scrc_v = {}, {}, {}, {}, {}
        for qb in range(2):
            q0 = qb * QB
            if qb == 0:
                T.wait("sG", g_kt)
                T.wait("sQ")
                SC.wait("sV")
            else:
                SC.wait("sP")
                SC.wait("sV")
                T.wait("sV")
                T.wait("sA")
                V.wait("sG", shl_ring[-1])
            qk_aring = []
            qk_pring = []
            for kc in range(32):
                P = pA if kc % 2 == 0 else pB
                if kc >= 2:
                    T.wait("sA", qk_aring[kc - 2])
                T.op(None, "matmul", P[:, 0:512], kTt[:, kc * 128:(kc + 1) * 128],
                     qT[:, q0:q0 + 512], start=True, stop=True)
                T.op("sP", "matmul", P[:, 512:1024], kTt[:, kc * 128:(kc + 1) * 128],
                     qT[:, q0 + 512:q0 + 1024], start=True, stop=True)
                qk_pring.append(cnt["sP"])
                if kc == 31:
                    T.fence("sP")
                pwait = qk_pring[kc] + 1 if kc < 31 else cnt["sP"]
                if kc in BAND_KC[qb]:
                    A_dev = 1024 + kc * 128 - qb * 1024
                    SY.wait("sV")
                    SY.dma("sQ", bandt[:, :], APc(W2.tensor, A_dev, [[1, 128], [1, QB]]))
                    SY.dma("sQ", stats[:, 4:8], bandt[:, 1020:1024])
                    V.wait("sQ")
                    V.wait("sP", pwait)
                    if len(qk_aring) >= 1:
                        V.wait("sA", qk_aring[-1])
                    bst = bandt.ap().ap[0][0]
                    revv = APc(bandt.ap().tensor, 1023, [[bst, 128], [-1, QB]])
                    V.op("sV", "tensor_add", out=scr[:, :], in0=P[:, :], in1=revv)
                    V.fence("sV")
                    SC.wait("sV")
                    SC.op("sA", "activation", out=scr2[:, :], in_=scr[:, :], func=AF.Relu,
                          scale=cisq[:, :])
                else:
                    SC.wait("sP", pwait)
                    SC.op("sA", "activation", out=scr2[:, :], in_=P[:, :], func=AF.Relu,
                          scale=cisq[:, :], bias=cbias[:, :])
                SC.wait("sA")
                SC.op("sA", "activation", out=kerT[:, kc * QB:(kc + 1) * QB], in_=scr2[:, :],
                      func=AF.Square)
                qk_aring.append(cnt["sA"])
            SC.fence("sA")
            av_vring = []
            vkc_ring = []
            for g in range(8):
                Pg = (pA, pB) if g % 2 == 0 else (pC, pD)
                if g >= 1:
                    SY.wait("sV", av_vring[g - 1])
                for sub in range(2):
                    ecc = g * 2 + sub
                    SY.dma("sQ", ustage[sub][:, :], UT[ecc * 128:(ecc + 1) * 128, q0:q0 + QB])
                if g == 0 and qb == 0:
                    SC.wait("sG", g_vok)
                if g == 0:
                    T.wait("sA")
                if g >= 2:
                    T.wait("sV", av_vring[g - 2])
                vload_cnt = []
                def vload(kcl, gg=g):
                    if vkc_ring:
                        SC.wait("sP", vkc_ring[-1])
                    SC.dma("sR", vstage[kcl % 4][:, :],
                           SHV[kcl * 128:(kcl + 1) * 128, gg * 256:(gg + 1) * 256])
                    vload_cnt.append(cnt["sR"])
                for kcl in range(3):
                    vload(kcl)
                for kc in range(32):
                    vs = vstage[kc % 4]
                    if kc + 3 < 32:
                        vload(kc + 3)
                        if kc + 3 == 31:
                            SC.dma("sR", tailb[:, 0:4], vstage[31 % 4][:, 252:256])
                    T.wait("sR", vload_cnt[kc + 1] if kc + 1 < 32 else cnt["sR"])
                    for sub in range(2):
                        P = Pg[sub]
                        st, sp = kc == 0, kc == 31
                        T.op(None, "matmul", P[:, 0:512], vs[:, sub * 128:(sub + 1) * 128],
                             kerT[:, kc * QB:kc * QB + 512], start=st, stop=sp)
                        T.op("sP" if sub == 1 else None, "matmul",
                             P[:, 512:1024], vs[:, sub * 128:(sub + 1) * 128],
                             kerT[:, kc * QB + 512:kc * QB + 1024], start=st, stop=sp)
                    vkc_ring.append(cnt["sP"])
                if g == 7:
                    T.fence("sP")
                    V.wait("sP")
                else:
                    V.wait("sP", vkc_ring[-1] + 1)
                V.wait("sQ")
                for sub in range(2):
                    ecc = g * 2 + sub
                    V.op("sV" if sub == 1 else None, "tensor_mul",
                         out=uaq[:, ecc * QB:(ecc + 1) * QB],
                         in0=Pg[sub][:, :], in1=ustage[sub][:, :])
                V.fence("sV")
                av_vring.append(cnt["sV"])
            if _UPTO == "av" and qb == 0:
                V.wait("sV")
                tiles = [uaq.ap().bitcast(F32)[:, i * 1024:(i + 1) * 1024] for i in range(8)]
                tiles += [kerT.ap().bitcast(F32)[:, i * 1024:(i + 1) * 1024] for i in range(8)]
                dump_and_end(tiles)
                return
            op_vring = []
            T.wait("sV")
            if qb == 0:
                T.wait("sG", g_ow)

            def logits_pe(j, flush):
                xsl = xl2r[j % 3]
                T.wait("sV", xls_v[j])
                if j >= 1:
                    T.wait("sA", xlt_a[j - 1])
                for dc in range(8):
                    T.op("sP" if dc == 7 else None, "transpose",
                         rr(pC.ap().bitcast(BF16)[:, 0:1024], "p (a b) -> p a b", a=8)[:, dc, :],
                         xsl[:, dc * 128:(dc + 1) * 128], idb16[:, :])
                T.fence("sP")
                tp_p[j] = cnt["sP"]
                SC.wait("sP", tp_p[j])
                SC.op("sA", "activation", out=xlt[:, :], in_=pC.ap().bitcast(BF16)[:, 0:1024],
                      func=AF.Copy)
                SC.fence("sA")
                xlt_a[j] = cnt["sA"]
                T.wait("sA", xlt_a[j])
                if j >= 1:
                    T.wait("sV", scrc_v[j - 1])
                for dc in range(8):
                    T.op("sP" if dc == 7 else None, "matmul", pD[:, 0:8],
                         xlt[:, dc * 128:(dc + 1) * 128], gatewb[:, dc * 8:(dc + 1) * 8],
                         start=(dc == 0), stop=(dc == 7))
                gate_p[j] = cnt["sP"]
                if flush:
                    T.fence("sP")
                    gate_p[j] = cnt["sP"]

            def logits_v(j, flush):
                V.wait("sP", gate_p[j] if flush else gate_p[j] + 1)
                if shl_ring:
                    V.wait("sG", shl_ring[-1])
                V.op("sV", "tensor_copy", out=scr[:, 0:8], in_=pD[:, 0:8])
                V.fence("sV")
                scrc_v[j] = cnt["sV"]
                GP.wait("sV")
                par_dma("sG", (SHL[j * 128:(j + 1) * 128, :],
                               SHL[HALF + j * 128:HALF + (j + 1) * 128, :]),
                        scr[:, 0:8])
                shl_ring.append(cnt["sG"])

            for tcl in range(8):
                tc = qb * 8 + tcl
                P = pA if tcl % 2 == 0 else pB
                buf = xt[tcl % 2]
                if tcl >= 2:
                    SY.wait("sV", op_vring[tcl - 2])
                    T.wait("sV", op_vring[tcl - 2])
                SY.dma("sQ", buf[:, :], X[tc * 128:(tc + 1) * 128, :])
                for ecc in range(16):
                    st, sp = ecc == 0, ecc == 15
                    lhs = uaq[:, ecc * QB + tcl * 128:ecc * QB + (tcl + 1) * 128]
                    T.op(None, "matmul", P[:, 0:512], lhs,
                         owts[:, ecc * 1024:ecc * 1024 + 512], start=st, stop=sp)
                    T.op("sP" if sp else None, "matmul", P[:, 512:1024], lhs,
                         owts[:, ecc * 1024 + 512:(ecc + 1) * 1024], start=st, stop=sp)
                T.fence("sP")
                ecc_f = cnt["sP"]
                if tcl >= 2:
                    logits_pe(tc - 2, False)
                V.wait("sP", ecc_f)
                V.wait("sQ")
                if shl_ring:
                    V.wait("sG", shl_ring[-1])
                V.op(None, "tensor_scalar", out=scr[:, :], in0=buf[:, :],
                     scalar1=mcol[:, tc:tc + 1], scalar2=rcol[:, tc:tc + 1],
                     op0=ALU.subtract, op1=ALU.mult)
                V.op(None, "tensor_mul", out=scr[:, :], in0=scr[:, :], in1=resb[:, :])
                V.op(None, "tensor_add", out=scr[:, :], in0=scr[:, :], in1=buf[:, :])
                V.op("sV", "tensor_add", out=evac[:, 0:1024], in0=scr[:, :], in1=P[:, :])
                V.fence("sV")
                op_vring.append(cnt["sV"])
                SY.wait("sV")
                SY.dma("sQ", X2D[tc * 128:(tc + 1) * 128, :], evac[:, 0:1024])
                SC.wait("sV")
                SC.op("sA", "activation", out=scr2[:, :], in_=evac[:, 0:1024], func=AF.Square)
                SC.fence("sA")
                V.op(None, "reduce_sum", out=m2col[:, tc:tc + 1], in_=evac[:, 0:1024], axis=AX.X)
                V.wait("sA")
                V.op(None, "reduce_sum", out=r2col[:, tc:tc + 1], in_=scr2[:, :], axis=AX.X)
                V.op(None, "tensor_scalar_mul", out=m2col[:, tc:tc + 1], in0=m2col[:, tc:tc + 1],
                     scalar1=1.0 / D)
                V.op(None, "tensor_scalar_mul", out=r2col[:, tc:tc + 1], in0=r2col[:, tc:tc + 1],
                     scalar1=1.0 / D)
                V.op(None, "tensor_mul", out=cum[:, tc:tc + 1], in0=m2col[:, tc:tc + 1],
                     in1=m2col[:, tc:tc + 1])
                V.op("sV", "tensor_sub", out=r2col[:, tc:tc + 1], in0=r2col[:, tc:tc + 1],
                     in1=cum[:, tc:tc + 1])
                V.fence("sV")
                SC.wait("sV")
                SC.op(None, "activation", out=tmp16[:, tc:tc + 1], in_=r2col[:, tc:tc + 1],
                      func=AF.Ln, bias=epsb[:, :])
                SC.op("sA", "activation", out=r2col[:, tc:tc + 1], in_=tmp16[:, tc:tc + 1],
                      func=AF.Exp, scale=cm05[:, :])
                SC.fence("sA")
                V.wait("sA")
                if tc >= 3:
                    V.wait("sP", tp_p[tc - 3])
                    V.wait("sQ", xq_ring[tc - 3])
                V.op("sV", "tensor_scalar", out=xl2r[tc % 3][:, :], in0=evac[:, 0:1024],
                     scalar1=m2col[:, tc:tc + 1], scalar2=r2col[:, tc:tc + 1],
                     op0=ALU.subtract, op1=ALU.mult)
                V.fence("sV")
                xls_v[tc] = cnt["sV"]
                SY.wait("sV", xls_v[tc])
                SY.dma("sQ", XL2D[tc * 128:(tc + 1) * 128, :], xl2r[tc % 3][:, :])
                xq_ring.append(cnt["sQ"])
                if tcl >= 2:
                    logits_v(tc - 2, False)
            logits_pe(qb * 8 + 6, False)
            logits_v(qb * 8 + 6, False)
            logits_pe(qb * 8 + 7, True)
            logits_v(qb * 8 + 7, True)
        if _UPTO == "x2":
            for tc in range(NTC):
                bx = xt[tc % 2]
                SY.dma("sQ", bx[:, :], X2D[tc * 128:(tc + 1) * 128, :])
                SY.wait("sQ")
                SY.dma("sQ", Y[tc * 128:(tc + 1) * 128, :], bx[:, :])
                SY.wait("sQ")
            GP.wait("sQ")
            return
        GP.wait("sG")
        par_dma("sG", (APc(FLG.tensor, 4, [[8, 1], [1, 2]]),
                       APc(FLG.tensor, 12, [[8, 1], [1, 2]])),
                tags[:, 4:6])
        if GP.h is not None:
            psrcf2 = sym_dst(FLG.tensor, (1 - hpar) * 8 + 4, [[8, 1], [1, 2]])
            exp_tag2 = GP.h.value_load(tags.ap()[0:1, 4:5])

            def poll2():
                GP.h.reg_add(kreg, kreg, 16)
                GP.h.dma_start(out=fbuf[:, 2:4], in_=psrcf2).then_inc(sems["sF"], 16)
                GP.h.wait_ge(sems["sF"], kreg)
                val = GP.h.value_load(fbuf.ap()[0:1, 2:3])
                return (val - exp_tag2) != 0

            with GP.h.While(poll2):
                GP.h.nop()
        GP.dma("sG", rr(tlog.ap(), "p (c e) -> p c e", c=32), rr(SHL, "(c p) e -> p c e", p=128))
        GP.dma("sG", stats[:, 0:4], tlog[:, 252:256])
        V.wait("sG")
        V.op("sV", "tensor_scalar_max", out=tlog[:, :], in0=tlog[:, :], scalar1=1e-6)
        V.fence("sV")
        SC.wait("sV")
        SC.op("sA", "activation", out=tlog[:, :], in_=tlog[:, :], func=AF.Ln)
        SC.fence("sA")
        SC.wait("sA")
        tview = rr(tlog.ap(), "p (c e) -> p c e", c=32)
        for it in range(8):
            SC.wait("sV")
            SC.op("sA", "activation", out=texp[:, :], in_=tlog[:, :], func=AF.Exp)
            SC.fence("sA")
            V.wait("sA")
            V.op("sV", "tensor_reduce", out=red8[:, :],
                 in_=rr(texp.ap(), "p (c e) -> p e c", c=32), axis=AX.X, op=ALU.add)
            V.fence("sV")
            T.wait("sV")
            T.op("sP", "matmul", pD[0:1, 0:8], onesc[:, :], red8[:, :], start=True, stop=True)
            T.fence("sP")
            SC.wait("sP")
            SC.op("sA", "activation", out=row8[:, :], in_=pD[0:1, 0:8], func=AF.Ln)
            SC.fence("sA")
            T.wait("sA")
            T.op("sP", "matmul", pD[:, 8:16], ones1[0:1, :], row8[0:1, :], start=True, stop=True)
            T.fence("sP")
            V.wait("sP")
            V.op(None, "tensor_copy", out=mid128[:, :], in_=pD[:, 8:16])
            V.op("sV", "tensor_sub", out=tview, in0=tview,
                 in1=mid128.ap().unsqueeze(1).broadcast_to([128, 32, 8]))
            V.fence("sV")
            SC.wait("sV")
            SC.op("sA", "activation", out=texp[:, :], in_=tlog[:, :], func=AF.Exp)
            SC.fence("sA")
            V.wait("sA")
            V.op("sV", "tensor_reduce", out=red32[:, :], in_=tview if False else
                 rr(texp.ap(), "p (c e) -> p c e", c=32), axis=AX.X, op=ALU.add)
            V.fence("sV")
            SC.wait("sV")
            SC.op("sA", "activation", out=red32[:, :], in_=red32[:, :], func=AF.Ln)
            SC.fence("sA")
            V.wait("sA")
            V.op("sV", "tensor_sub", out=tview, in0=tview,
                 in1=red32.ap().unsqueeze(2).broadcast_to([128, 32, 8]))
            V.fence("sV")
        SC.wait("sV")
        SC.op("sA", "activation", out=texp[:, :], in_=tlog[:, :], func=AF.Exp)
        SC.fence("sA")
        V.wait("sA")
        V.op(None, "memset", lob[:, :], 0.0)
        V.op("sV", "memset", hib[:, :], 1.0)
        kfr3 = rr(kfrac.ap(), "p (k e) -> p k e", k=15)
        cnd3 = rr(candr.ap(), "p (k e) -> p k e", k=15)
        v8 = lambda ap: ap.unsqueeze(1).broadcast_to([1, 15, 8])
        for rnd in range(7):
            V.op(None, "tensor_sub", out=t8a[:, :], in0=hib[:, :], in1=lob[:, :])
            V.op(None, "tensor_tensor", out=cnd3, in0=kfr3, in1=v8(t8a.ap()), op=ALU.mult)
            V.op("sV", "tensor_tensor", out=cnd3, in0=cnd3, in1=v8(lob.ap()), op=ALU.add)
            V.fence("sV")
            T.wait("sV")
            T.op("sP", "matmul", pD[:, 0:120], ones1[0:1, :], candr[0:1, :], start=True, stop=True)
            T.fence("sP")
            V.wait("sP")
            for k in range(15):
                V.op(None, "tensor_tensor", out=rr(tlog.ap(), "p (c e) -> p c e", c=32),
                     in0=rr(texp.ap(), "p (c e) -> p c e", c=32),
                     in1=pD[:, k * 8:(k + 1) * 8].unsqueeze(1).broadcast_to([128, 32, 8]),
                     op=ALU.is_gt)
                V.op("sV" if k == 14 else None, "tensor_reduce", out=scr2[:, k * 8:(k + 1) * 8],
                     in_=rr(tlog.ap(), "p (c e) -> p e c", c=32), axis=AX.X, op=ALU.add)
            V.fence("sV")
            T.wait("sV")
            T.op("sP", "matmul", pD[0:1, 0:120], onesc[:, :], scr2[:, 0:120], start=True, stop=True)
            T.fence("sP")
            V.wait("sP")
            V.op(None, "tensor_copy", out=rowK[:, :], in_=pD[0:1, 0:120])
            V.op(None, "tensor_single_scalar", out=gK[:, :], in_=rowK[:, :], scalar=511.5, op=ALU.is_ge)
            V.op(None, "tensor_mul", out=rowK[:, :], in0=candr[:, :], in1=gK[:, :])
            V.op(None, "tensor_reduce", out=t8a[:, :],
                 in_=rr(rowK.ap(), "p (k e) -> p e k", k=15), axis=AX.X, op=ALU.max)
            V.op(None, "tensor_tensor", out=lob[:, :], in0=lob[:, :], in1=t8a[:, :], op=ALU.max)
            V.op(None, "tensor_scalar_mul", out=cK2[:, :], in0=gK[:, :], scalar1=2.0)
            V.op(None, "tensor_tensor", out=cK2[:, :], in0=cK2[:, :], in1=candr[:, :], op=ALU.add)
            V.op(None, "tensor_tensor", out=rowK[:, :], in0=cK2[:, :], in1=gK[:, :], op=ALU.mult)
            V.op(None, "tensor_sub", out=cK2[:, :], in0=cK2[:, :], in1=rowK[:, :])
            V.op(None, "tensor_scalar_mul", out=rowK[:, :], in0=gK[:, :], scalar1=2.0)
            V.op(None, "tensor_add", out=cK2[:, :], in0=cK2[:, :], in1=rowK[:, :])
            V.op(None, "tensor_reduce", out=t8a[:, :],
                 in_=rr(cK2.ap(), "p (k e) -> p e k", k=15), axis=AX.X, op=ALU.min)
            V.op("sV", "tensor_tensor", out=hib[:, :], in0=hib[:, :], in1=t8a[:, :], op=ALU.min)
        V.fence("sV")
        T.wait("sV")
        T.op("sP", "matmul", pD[:, 8:16], ones1[0:1, :], lob[0:1, :], start=True, stop=True)
        T.fence("sP")
        V.wait("sP")
        V.op("sV", "tensor_copy", out=mid128[:, :], in_=pD[:, 8:16])
        V.fence("sV")
        GP.wait("sV")
        par_dma("sG", (gown[:, :], gown[:, :]), None,
                src01=(texp[:, 0:128], texp[:, 128:256]))
        GP.dma("sG", stats[:, 0:4], gown[:, 124:128])
        V.wait("sG")
        govw = rr(gown.ap(), "p (c e) -> p c e", c=16)
        V.op(None, "tensor_tensor", out=rr(selo.ap(), "p (c e) -> p c e", c=16), in0=govw,
             in1=mid128.ap().unsqueeze(1).broadcast_to([128, 16, 8]), op=ALU.is_gt)
        V.op(None, "tensor_single_scalar", out=hrdo[:, :], in_=gown[:, :], scalar=0.5, op=ALU.is_gt)
        V.op(None, "memset", accm[:, :], 0.0)
        sel3 = rr(selo.ap(), "p (c e) -> p c e", c=16)
        win3 = rr(wino.ap(), "p (c e) -> p c e", c=16)
        hrd3 = rr(hrdo.ap(), "p (c e) -> p c e", c=16)
        for e8 in range(7, -1, -1):
            V.op(None, "tensor_scalar", out=tmp16[:, :], in0=accm[:, :], scalar1=-1.0, scalar2=1.0,
                 op0=ALU.mult, op1=ALU.add)
            V.op(None, "tensor_tensor", out=win3[:, :, e8], in0=sel3[:, :, e8], in1=tmp16[:, :], op=ALU.mult)
            V.op(None, "tensor_tensor", out=accm[:, :], in0=accm[:, :], in1=sel3[:, :, e8], op=ALU.max)
        V.op(None, "memset", hwcol[:, :], 0.0)
        for e8 in range(8):
            V.op(None, "tensor_tensor", out=tmp16[:, :], in0=win3[:, :, e8], in1=hrd3[:, :, e8], op=ALU.mult)
            V.op(None, "tensor_add", out=hwcol[:, :], in0=hwcol[:, :], in1=tmp16[:, :])
        V.op("sV", "tensor_reduce", out=cnts[:, :], in_=rr(wino.ap(), "p (c e) -> p e c", c=16),
             axis=AX.X, op=ALU.add)
        V.fence("sV")
        T.wait("sV")
        T.op("sP", "matmul", pD[:, 8:16], tris[:, :], cnts[:, :], start=True, stop=True)
        T.fence("sP")
        V.wait("sP")
        V.op(None, "tensor_copy", out=pref[:, :], in_=pD[:, 8:16])
        V.op(None, "memset", slots[:, :], 0.0)
        for e8 in range(8):
            V.op(None, "tensor_tensor_scan", out=cum[:, :], data0=win3[:, :, e8], data1=win3[:, :, e8],
                 initial=0.0, op0=ALU.add, op1=ALU.bypass)
            V.op(None, "tensor_sub", out=cum[:, :], in0=cum[:, :], in1=win3[:, :, e8])
            V.op(None, "tensor_scalar", out=cum[:, :], in0=cum[:, :], scalar1=pref[:, e8:e8 + 1],
                 scalar2=float(e8 * 512), op0=ALU.add, op1=ALU.add)
            V.op(None, "tensor_tensor", out=cum[:, :], in0=cum[:, :], in1=win3[:, :, e8], op=ALU.mult)
            V.op(None, "tensor_add", out=slots[:, :], in0=slots[:, :], in1=cum[:, :])
        V.op(None, "tensor_scalar", out=tmp16[:, :], in0=accm[:, :], scalar1=-OOBF, scalar2=OOBF,
             op0=ALU.mult, op1=ALU.add)
        V.op(None, "tensor_add", out=slots[:, :], in0=slots[:, :], in1=tmp16[:, :])
        V.op("sV", "tensor_copy", out=sloti[:, :], in_=slots[:, :])
        V.fence("sV")
        if not gC:
            gB["_es"].close()
            esC = contextlib.ExitStack()
            gC["gTt0"] = esC.enter_context(nc.sbuf_tensor("gTt0", [128, 8 * 512], BF16))
            gC["gTt1"] = esC.enter_context(nc.sbuf_tensor("gTt1", [128, 8 * 512], BF16))
            gC["wep0"] = esC.enter_context(nc.sbuf_tensor("wep0", [128, D], BF16))
            gC["wep1"] = esC.enter_context(nc.sbuf_tensor("wep1", [128, D], BF16))
            gC["orow"] = esC.enter_context(nc.sbuf_tensor("orow", [128, D], F32))
            gC["oid"] = esC.enter_context(nc.sbuf_tensor("oidt", [128, 8], BF16))
            gC["ftile0"] = esC.enter_context(nc.sbuf_tensor("ftile0", [128, D], F32))
            gC["ftile1"] = esC.enter_context(nc.sbuf_tensor("ftile1", [128, D], F32))
        gTt = [gC["gTt0"], gC["gTt1"]]
        wep = [gC["wep0"], gC["wep1"]]
        orow, oid = gC["orow"], gC["oid"]
        ftile = [gC["ftile0"], gC["ftile1"]]
        rs_ring = []
        for c16 in range(NTC):
            fb = ftile[c16 % 2]
            if c16 >= 2:
                SY.wait("sV", rs_ring[c16 - 2])
            SY.dma("sQ", fb.ap().bitcast(BF16)[:, 0:1024], XL2D[c16 * 128:(c16 + 1) * 128, :])
            SY.dma("sQ", tailb[:, 8:12], fb.ap().bitcast(BF16)[:, 1020:1024])
            V.wait("sQ")
            if c16 >= 1:
                V.wait("sG")
            V.op(None, "tensor_scalar", out=spay[:, 0:1024], in0=fb.ap().bitcast(BF16)[:, 0:1024],
                 scalar1=hwcol[:, c16:c16 + 1], scalar2=None, op0=ALU.mult)
            V.op("sV", "tensor_copy", out=spay[:, 1024:1026], in_=idbf[:, 2 * c16:2 * c16 + 2])
            V.fence("sV")
            rs_ring.append(cnt["sV"])
            GP.wait("sV")
            GP.ind("sG", GBUF[:, :], IndirectOffsetOnAxis(sloti[:, c16:c16 + 1], 0),
                   spay[:, :], None, bcreg)
        GP.dma("sG", tailb[0:1, 4:8], GBUF[0:1, 0:4])
        scat_g = cnt["sG"]
        if _UPTO == "gbuf":
            GP.wait("sG")
            for tc in range(NTC):
                bx = xt[tc % 2]
                SY.wait("sG")
                SY.dma("sQ", bx.ap().bitcast(BF16)[:, 0:1028], GBUF[tc * 128:(tc + 1) * 128, :])
                SY.wait("sQ")
                SY.dma("sQ", Y[tc * 128:(tc + 1) * 128, :], bx[:, :])
                SY.wait("sQ")
            GP.wait("sQ")
            return
        T.wait("sV")
        ffn_vring = []
        ffn_evac_ring = []
        for e8 in range(8):
            gt = gTt[e8 % 2]
            if e8 >= 2:
                SC.wait("sP", ffn_vring[e8 - 2])
            if e8 >= 1:
                T.wait("sV", ffn_evac_ring[e8 - 1])
            SC.wait("sG", scat_g)
            for dc in range(8):
                SC.dma("sR", gt[:, dc * 512:(dc + 1) * 512],
                       GBUF[e8 * 512:(e8 + 1) * 512, dc * 128:(dc + 1) * 128], transpose=True)
            SC.dma("sR", tailb[:, 0:4], gt[:, 8 * 512 - 4:8 * 512])
            T.wait("sR")
            ic_ring = []
            for ic in range(8):
                wb = wep[ic % 2]
                if ic >= 2:
                    GP.wait("sP", ic_ring[ic - 2])
                GP.dma("sG", wb[:, :], EXPERTS[e8, ic * 128:(ic + 1) * 128, :])
                GP.dma("sG", tailb[:, 4:8], wb[:, 1020:1024])
                T.wait("sG")
                for g4 in range(4):
                    P = (pA, pB, pC, pD)[g4]
                    st, sp = ic == 0, ic == 7
                    lhs = gt[:, ic * 512 + g4 * 128:ic * 512 + (g4 + 1) * 128]
                    T.op(None, "matmul", P[:, 0:512], lhs, wb[:, 0:512], start=st, stop=sp)
                    T.op("sP" if g4 == 3 else None, "matmul",
                         P[:, 512:1024], lhs, wb[:, 512:1024], start=st, stop=sp)
                ic_ring.append(cnt["sP"])
            T.fence("sP")
            ffn_vring.append(cnt["sP"])
            V.wait("sP")
            V.wait("sQ")
            for g4 in range(4):
                P = (pA, pB, pC, pD)[g4]
                V.op("sV", "tensor_copy", out=orow[:, :], in_=P[:, :])
                V.fence("sV")
                SY.wait("sV")
                SY.dma("sQ", OBUF[e8 * 512 + g4 * 128:e8 * 512 + (g4 + 1) * 128, :], orow[:, :])
                if g4 < 3:
                    V.wait("sQ")
            ffn_evac_ring.append(cnt["sV"])
        SY.dma("sQ", stats[0:4, 4:8], OBUF[4092:4096, 1020:1024])
        GP.wait("sQ")
        allwait("sync", "vector", "gpsimd")
        p8_ring = []
        for tc in range(NTC):
            fb = ftile[tc % 2]
            bx = xt[tc % 2]
            if tc >= 2:
                V.wait("sQ")
                SY.wait("sV", p8_ring[tc - 2])
                GP.wait("sV", p8_ring[tc - 2])
            GP.dma("sG", fb[:, :], zzf[:, :])
            GP.ind("sG", fb[:, :], None, OBUF[:, :],
                   IndirectOffsetOnAxis(sloti[:, tc:tc + 1], 0), bcreg)
            GP.dma("sG", stats[:, 0:4], fb[:, 1020:1024])
            SY.dma("sQ", bx[:, :], X2D[tc * 128:(tc + 1) * 128, :])
            SY.dma("sQ", stats[:, 4:8], bx[:, 1020:1024])
            V.wait("sQ")
            V.wait("sG")
            V.op("sV", "tensor_add", out=evac[:, 0:1024], in0=fb[:, :], in1=bx[:, :])
            V.fence("sV")
            p8_ring.append(cnt["sV"])
            SY.wait("sV")
            SY.dma("sQ", Y[tc * 128:(tc + 1) * 128, :], evac[:, 0:1024])
        GP.wait("sQ")
        if _DUMPT is not None:
            for sm in ("sV", "sA", "sP", "sG", "sR", "sQ"):
                SY.wait(sm)
            if _DUMPT == "STATS":
                for j, tl in enumerate((mcol, rcol, scol)):
                    SY.dma("sQ", Y[0:128, j * 16:(j + 1) * 16], tl[:, :])
                SY.wait("sQ")
                GP.wait("sQ")
                state["final_sQ"] = cnt["sQ"]
                return
            srcs = {"UT": (UT, "bf"), "X2D": (X2D, "f32"), "XL2D": (XL2D, "bfn"),
                    "OBUF": (OBUF, "f32"), "OBUF2": (OBUF, "f32h"),
                    "GBUF": (GBUF, "gb"), "XNT": (xnT, "sb")}
            src, fmt = srcs[_DUMPT]
            for tc in range(NTC):
                bx = xt[tc % 2]
                if fmt == "f32":
                    SY.dma("sQ", bx[:, :], src[tc * 128:(tc + 1) * 128, 0:1024])
                elif fmt == "f32h":
                    SY.dma("sQ", bx[:, :], src[2048 + tc * 128:2048 + (tc + 1) * 128, 0:1024])
                elif fmt == "bf":
                    SY.dma("sQ", bx.ap().bitcast(BF16)[:, 0:2048], src[tc * 128:(tc + 1) * 128, 0:2048])
                elif fmt == "bfn":
                    SY.dma("sQ", bx.ap().bitcast(BF16)[:, 0:1024], src[tc * 128:(tc + 1) * 128, 0:1024])
                    SY.dma("sQ", bx.ap().bitcast(BF16)[:, 1024:2048], src[tc * 128:(tc + 1) * 128, 0:1024])
                elif fmt == "gb":
                    SY.dma("sQ", bx.ap().bitcast(BF16)[:, 0:1028], src[tc * 128:(tc + 1) * 128, 0:1028])
                elif fmt == "sb":
                    if tc < 8:
                        SY.dma("sQ", bx[:, :], src.ap().bitcast(F32)[:, tc * 1024:(tc + 1) * 1024])
                    else:
                        SY.dma("sQ", bx[:, :], X2D[tc * 128:(tc + 1) * 128, 0:1024])
                SY.wait("sQ")
                SY.dma("sQ", Y[tc * 128:(tc + 1) * 128, :], bx[:, :])
                SY.wait("sQ")
            GP.wait("sQ")
        state["final_sQ"] = cnt["sQ"]

    with nc.Block() as block:
        regs = {"sync": block.sync, "gpsimd": block.gpsimd, "vector": block.vector,
                "tensor": block.tensor, "scalar": block.scalar}
        for en in ENGS:
            def body(h, en=en):
                cnt = {}
                chains = {"vector": "sV", "scalar": "sA"}
                px = {e: Pxy(cnt, sems, h if e == en else None, chains.get(e)) for e in ENGS}
                try:
                    program(px)
                except Exception:
                    import traceback
                    print("FAILED in engine pass:", en)
                    traceback.print_exc()
                    raise
            regs[en](body)
    return nc


_NC_CACHE = {}


def _host_inputs(x, g, uv_w, gamma, beta, w_rel, o_w, res_scale, gate_w, experts):
    f32 = np.float32
    uvwT = np.ascontiguousarray(uv_w.T).astype(f32, copy=False)
    owT = np.ascontiguousarray(o_w.T).astype(f32, copy=False)
    gb4 = np.ascontiguousarray(np.stack([gamma[0], beta[0], gamma[1], beta[1]], axis=1)).astype(f32)
    gsc = np.array([[float(g[0]), 0.0]], f32)
    ress = res_scale.reshape(1, D).astype(f32, copy=False)
    inv_freq = (10000.0 ** (np.arange(64, dtype=np.float64) / 64)).astype(f32)
    pos = np.arange(L, dtype=f32)[:, None]
    sinu = (pos * inv_freq[None, :]).astype(f32)
    SINf = np.sin(sinu).astype(f32)
    COSf = np.cos(sinu).astype(f32)
    def w2_for(h):
        m = np.arange(4608) + (1 - h) * 2048
        return w_rel[np.clip(m, 0, 2046)].astype(f32).reshape(1, 4608)
    ids_lo = (np.arange(NTC)[None, :] * 128 + np.arange(128)[:, None]).astype(np.int32)
    tri = (np.arange(128)[:, None] < np.arange(128)[None, :]).astype(f32)
    ident = np.eye(128, dtype=f32)
    tag = np.random.randint(1 << 20, 1 << 30, size=3).astype(np.int32)
    tags = np.array([[tag[0], 0, tag[1], 0, tag[2], 0]], np.int32)
    ins = []
    for c in range(8):
        b, h = c // 2, c % 2
        t0 = h * HALF
        ins.append({
            "x_own": np.ascontiguousarray(x[b, t0:t0 + HALF, :]).astype(f32, copy=False),
            "uvwT": uvwT,
            "owT": owT,
            "experts": experts.astype(f32, copy=False),
            "gateW": np.ascontiguousarray(gate_w).astype(f32, copy=False),
            "gb4": gb4,
            "gsc": gsc,
            "res_scale": ress,
            "w2": w2_for(h),
            "cosT": np.ascontiguousarray(COSf[t0:t0 + HALF, :].T),
            "sinT": np.ascontiguousarray(SINf[t0:t0 + HALF, :].T),
            "ids_lo": ids_lo,
            "tags": tags,
            "tri": tri,
            "ident": ident,
        })
    return ins


def kernel(L0_attn_x, L0_attn_g, L0_attn_uv_w, L0_attn_gamma, L0_attn_beta,
           L0_attn_w_rel, L0_attn_o_w, L0_attn_res_scale, L0_ffn_gate_w,
           L0_ffn_experts):
    x = np.asarray(L0_attn_x, np.float32)
    ins = _host_inputs(x, np.asarray(L0_attn_g, np.float32),
                       np.asarray(L0_attn_uv_w, np.float32),
                       np.asarray(L0_attn_gamma, np.float32),
                       np.asarray(L0_attn_beta, np.float32),
                       np.asarray(L0_attn_w_rel, np.float32),
                       np.asarray(L0_attn_o_w, np.float32),
                       np.asarray(L0_attn_res_scale, np.float32),
                       np.asarray(L0_ffn_gate_w, np.float32)[0],
                       np.asarray(L0_ffn_experts, np.float32))
    if "nc" not in _NC_CACHE:
        _NC_CACHE["nc"] = build()
    res = run_bass_kernel_spmd(_NC_CACHE["nc"], ins, core_ids=list(range(8)))
    out = np.empty((B, L, D), np.float32)
    for c in range(8):
        b, h = c // 2, c % 2
        out[b, h * HALF:(h + 1) * HALF, :] = res.results[c]["y_own"]
    return out


if __name__ == "__main__":
    nc = build()
    print("build ok:", nc.n_instructions(), "instructions")
